# version 12 (final)
# baseline (speedup 1.0000x reference)
import numpy as np
import ml_dtypes
from contextlib import ExitStack

import concourse.bass as bass
import concourse.mybir as mybir
from concourse.bass_utils import run_bass_kernel_spmd
from concourse.expressions import RuntimeValue
from concourse.aot_env import allow_ici_all_to_all

F32 = mybir.dt.float32
BF16 = mybir.dt.bfloat16
F16 = mybir.dt.float16
I32 = mybir.dt.int32
AF = mybir.ActivationFunctionType
ALU = mybir.AluOpType
ds = bass.ds

S, E, H, D = 8192, 2048, 16, 128
NI, NIT, ET = 5504, 43, 16
NCORE = 8
Q = 1024
QC = 512
NQK = S // QC
SCALE = 1.0 / float(np.sqrt(D))
EXPB = -12.0
NEG = -30000.0
EPS = 1e-5


class Ctr:
    __slots__ = ("h", "n")

    def __init__(self, nc, es, name):
        self.h = es.enter_context(nc.semaphore(name))
        self.n = 0

    def inc(self, instr, k=1):
        instr.then_inc(self.h, k)
        self.n += k
        return self.n


DEBUG = False


def build_nc():
    nc = bass.Bass("TRN2", target_bir_lowering=False, dynamic_dma_scratch_size=16384)

    xT_in = nc.dram_tensor("xT_in", [E, Q], F32, kind="ExternalInput")
    wqT = nc.dram_tensor("wqT", [E, E], F32, kind="ExternalInput")
    wkT = nc.dram_tensor("wkT", [E, E], F32, kind="ExternalInput")
    wvT = nc.dram_tensor("wvT", [E, E], F32, kind="ExternalInput")
    woT = nc.dram_tensor("woT", [E, E], F32, kind="ExternalInput")
    bq_in = nc.dram_tensor("bq_in", [128, 16], F32, kind="ExternalInput")
    bk_in = nc.dram_tensor("bk_in", [128, 16], F32, kind="ExternalInput")
    bv_in = nc.dram_tensor("bv_in", [128, 16], F32, kind="ExternalInput")
    bo_in = nc.dram_tensor("bo_in", [128, 16], F32, kind="ExternalInput")
    lut_in = nc.dram_tensor("lut_in", [128, 16], F32, kind="ExternalInput")
    gidx_in = nc.dram_tensor("gidx_in", [256, NI], I32, kind="ExternalInput")
    uidx_in = nc.dram_tensor("uidx_in", [256, NI], I32, kind="ExternalInput")
    didx_in = nc.dram_tensor("didx_in", [688, E], I32, kind="ExternalInput")
    cos_in = nc.dram_tensor("cos_in", [128, Q], BF16, kind="ExternalInput")
    sin_in = nc.dram_tensor("sin_in", [128, Q], BF16, kind="ExternalInput")
    tri_in = nc.dram_tensor("tri_in", [128, 128], F32, kind="ExternalInput")
    rot_in = nc.dram_tensor("rot_in", [128, 128], BF16, kind="ExternalInput")

    out_y = nc.dram_tensor("out_y", [E, Q], F32, kind="ExternalOutput")
    taps = {}
    if DEBUG:
        taps["xln1"] = nc.dram_tensor("tap_xln1", [128, ET * Q], BF16, kind="ExternalOutput")
        taps["x1"] = nc.dram_tensor("tap_x1", [128, ET * Q], F32, kind="ExternalOutput")
        taps["qkv"] = nc.dram_tensor("tap_qkv", [3 * E, Q], BF16, kind="ExternalOutput")
        taps["orr"] = nc.dram_tensor("tap_orr", [E, Q], BF16, kind="ExternalOutput")
        taps["x2"] = nc.dram_tensor("tap_x2", [128, ET * Q], F32, kind="ExternalOutput")
        taps["wg"] = nc.dram_tensor("tap_wg", [E, NI], BF16, kind="ExternalOutput")
        taps["h"] = nc.dram_tensor("tap_h", [128, NIT * QC], BF16, kind="ExternalOutput")
        taps["kt"] = nc.dram_tensor("tap_kt", [128, 2 * S], BF16, kind="ExternalOutput")
        taps["vt"] = nc.dram_tensor("tap_vt", [128, 2 * S], BF16, kind="ExternalOutput")
        taps["gate"] = nc.dram_tensor("tap_gate", [128, QC], F32, kind="ExternalOutput")
        taps["up"] = nc.dram_tensor("tap_up", [128, QC], F32, kind="ExternalOutput")
        taps["wd"] = nc.dram_tensor("tap_wd", [NI, E], BF16, kind="ExternalOutput")
        taps["x1b"] = nc.dram_tensor("tap_x1b", [128, ET * Q], F32, kind="ExternalOutput")

    EH = E // 2
    ag_in = {"g": nc.dram_tensor("ag_g_in", [256, NI], BF16, kind="Internal"),
             "u": nc.dram_tensor("ag_u_in", [256, NI], BF16, kind="Internal"),
             "d0": nc.dram_tensor("ag_d0_in", [688, EH], BF16, kind="Internal"),
             "d1": nc.dram_tensor("ag_d1_in", [688, EH], BF16, kind="Internal")}
    ag_out = {"g": nc.dram_tensor("ag_g_out", [E, NI], BF16, kind="Internal", addr_space="Shared"),
              "u": nc.dram_tensor("ag_u_out", [E, NI], BF16, kind="Internal", addr_space="Shared"),
              "d0": nc.dram_tensor("ag_d0_out", [NI, EH], BF16, kind="Internal", addr_space="Shared"),
              "d1": nc.dram_tensor("ag_d1_out", [NI, EH], BF16, kind="Internal", addr_space="Shared")}
    agkv_in = nc.dram_tensor("agkv_in", [NCORE * 512, Q], BF16, kind="Internal")
    agkv_out = nc.dram_tensor("agkv_out", [NCORE * 512, Q], BF16, kind="Internal")
    agq_in = nc.dram_tensor("agq_in", [NCORE * 256, Q], BF16, kind="Internal")
    agq_out = nc.dram_tensor("agq_out", [NCORE * 256, Q], BF16, kind="Internal")
    ago_in = nc.dram_tensor("ago_in", [E, Q], BF16, kind="Internal")
    ago_out = nc.dram_tensor("ago_out", [E, Q], BF16, kind="Internal")

    es = ExitStack()
    sb = lambda name, shp, dt: es.enter_context(nc.sbuf_tensor(name, shp, dt))

    xT = sb("xT", [128, ET * Q], F32)
    tri_sb = sb("tri_sb", [128, 128], F32)
    lut_sb = sb("lut_sb", [128, 16], F32)
    bias_sb = {p: sb(f"b{p}_sb", [128, 16], F32) for p in "qkvo"}
    ones_cb = sb("ones_cb", [128, 1], BF16)
    ones_ff = sb("ones_ff", [128, 128], F32)
    ones_rf = sb("ones_rf", [1, 128], F32)
    ident = sb("ident", [128, 128], BF16)
    rot_sb = sb("rot_sb", [128, 128], BF16)
    cos_sb = sb("cos_sb", [128, Q], BF16)
    sin_sb = sb("sin_sb", [128, Q], BF16)
    mu_sb = sb("mu_sb", [1, QC], F32)
    var_sb = sb("var_sb", [1, QC], F32)
    std_sb = sb("std_sb", [1, QC], F32)
    rs_sb = sb("rs_sb", [1, QC], F32)
    bvv_sb = sb("bvv_sb", [1, QC], F32)
    A_sb = sb("A_sb", [128, QC], F32)
    B_sb = sb("B_sb", [128, QC], F32)
    xnt_sb = sb("xnt_sb", [128, QC], F32)
    eps_sb = sb("eps_sb", [128, 1], F32)
    expb_sb = sb("expb_sb", [128, 1], F32)
    R_sb = sb("R_sb", [128, QC], F32)
    o_sb = sb("o_sb", [128, QC], BF16)
    la_sb = xnt_sb
    lb_sb = A_sb

    s_pe = Ctr(nc, es, "s_pe")
    s_dve = Ctr(nc, es, "s_dve")
    s_act = Ctr(nc, es, "s_act")
    d_sync = Ctr(nc, es, "d_sync")
    d_scal = Ctr(nc, es, "d_scal")
    d_gp = Ctr(nc, es, "d_gp")
    s_cc = Ctr(nc, es, "s_cc")
    d_pa = [Ctr(nc, es, "d_pa0"), Ctr(nc, es, "d_pa1")]
    d_sw = Ctr(nc, es, "d_sw")
    d_stq = [Ctr(nc, es, "d_stq0"), Ctr(nc, es, "d_stq1")]
    s_gp = Ctr(nc, es, "s_gp")
    d_mg = [Ctr(nc, es, "d_mg0"), Ctr(nc, es, "d_mg1")]
    d_gst = Ctr(nc, es, "d_gst")

    with nc.Block() as block:
        @block.sync
        def _(sy):
            for dst, src in ((tri_sb, tri_in), (lut_sb, lut_in), (cos_sb, cos_in),
                             (sin_sb, sin_in), (bias_sb["q"], bq_in), (bias_sb["k"], bk_in),
                             (bias_sb["v"], bv_in), (bias_sb["o"], bo_in),
                             (rot_sb, rot_in)):
                d_sync.inc(sy.dma_start(out=dst[:, :], in_=src[:, :]), 16)
            for g in range(4):
                d_sync.inc(sy.dma_start(
                    out=xT[:, :].rearrange("p (t q) -> p t q", q=Q)[:, 4 * g:4 * g + 4, :],
                    in_=xT_in[:, :].rearrange("(t p) q -> p t q", p=128)[:, 4 * g:4 * g + 4, :]), 16)
        x_ready = d_sync.n

        @block.gpsimd
        def _(gp):
            gp.memset(ones_cb[:, :], 1.0)
            gp.memset(ones_ff[:, :], 1.0)
            gp.memset(ones_rf[:, :], 1.0)
            s_dve.inc(gp.memset(eps_sb[:, :], EPS), 1)
            s_dve.inc(gp.memset(expb_sb[:, :], EXPB), 1)
            s_dve.inc(gp.memset(ident[:, :], 1.0), 1)
            gp.wait_ge(s_dve.h, s_dve.n)
            s_dve.inc(gp.affine_select(ident[:, :], ident[:, :], [[1, 128]],
                                       ALU.is_equal, 0.0, base=0, channel_multiplier=-1), 1)

        def emit_ln(tag, xln, x_ready_dve=None, x_ready_dma=None, x_ready_scal=None,
                    qcs=(0, 1)):
            pse = ExitStack()
            psM = pse.enter_context(nc.psum_tensor(f"psM{tag}", [1, QC], F32))
            psV = pse.enter_context(nc.psum_tensor(f"psV{tag}", [1, QC], F32))
            psA = pse.enter_context(nc.psum_tensor(f"psA{tag}", [128, QC], F32))
            psB = pse.enter_context(nc.psum_tensor(f"psB{tag}", [128, QC], F32))
            done = {}
            prev_var_dve = None
            for qc in qcs:
                def xsl(et, qc=qc):
                    return xT[:, et * Q + qc * QC: et * Q + qc * QC + QC]

                def lsl(et, qc=qc, xln=xln):
                    return xln[:, et * Q + qc * QC: et * Q + qc * QC + QC]

                pv = prev_var_dve

                @block.vector
                def _(v):
                    if x_ready_dve is not None:
                        v.wait_ge(s_dve.h, x_ready_dve)
                    if x_ready_dma is not None:
                        v.wait_ge(d_sync.h, x_ready_dma)
                    if x_ready_scal is not None:
                        v.wait_ge(d_scal.h, x_ready_scal)
                    last = None
                    for et in range(ET):
                        last = v.tensor_copy(lsl(et), xsl(et))
                    s_dve.inc(last, 1)
                cast_all = s_dve.n

                @block.tensor
                def _(t):
                    t.wait_ge(s_dve.h, cast_all)
                    if pv is not None:
                        t.wait_ge(s_dve.h, pv)
                    last = None
                    for et in range(ET):
                        last = t.matmul(psM[:, :], ones_cb[:, :], lsl(et),
                                        start=(et == 0), stop=(et == ET - 1))
                    s_pe.inc(last, 1)
                mean_pe = s_pe.n

                @block.vector
                def _(v):
                    v.wait_ge(s_pe.h, mean_pe)
                    last = None
                    for et in range(ET):
                        last = v.tensor_tensor(lsl(et), lsl(et), lsl(et), ALU.mult)
                    s_dve.inc(last, 1)
                sq_all = s_dve.n

                @block.tensor
                def _(t):
                    t.wait_ge(s_dve.h, sq_all)
                    last = None
                    for et in range(ET):
                        last = t.matmul(psV[:, :], ones_cb[:, :], lsl(et),
                                        start=(et == 0), stop=(et == ET - 1))
                    s_pe.inc(last, 1)
                stats_pe = s_pe.n

                @block.vector
                def _(v):
                    v.wait_ge(s_pe.h, stats_pe)
                    v.tensor_scalar(mu_sb[:, :], psM[:, :], 1.0 / E, None, ALU.mult)
                    v.tensor_scalar(var_sb[:, :], psV[:, :], 1.0 / E, None, ALU.mult)
                    v.tensor_tensor(bvv_sb[:, :], mu_sb[:, :], mu_sb[:, :], ALU.mult)
                    s_dve.inc(v.tensor_tensor(var_sb[:, :], var_sb[:, :], bvv_sb[:, :],
                                              ALU.subtract), 1)
                var_dve = s_dve.n
                prev_var_dve = var_dve

                @block.scalar
                def _(sc):
                    sc.wait_ge(s_dve.h, var_dve)
                    s_act.inc(sc.activation(std_sb[:, :], var_sb[:, :], AF.Sqrt,
                                            bias=eps_sb[0:1, :], scale=1.0), 1)
                std_act = s_act.n

                @block.vector
                def _(v):
                    v.wait_ge(s_act.h, std_act)
                    v.reciprocal(rs_sb[:, :], std_sb[:, :])
                    v.tensor_scalar(bvv_sb[:, :], mu_sb[:, :], -1.0, None, ALU.mult)
                    s_dve.inc(v.tensor_tensor(bvv_sb[:, :], bvv_sb[:, :], rs_sb[:, :],
                                              ALU.mult), 1)
                ab_dve = s_dve.n

                @block.tensor
                def _(t):
                    t.wait_ge(s_dve.h, ab_dve)
                    t.matmul(psA[:, :], ones_rf[:, :], rs_sb[:, :], start=True, stop=True)
                    s_pe.inc(t.matmul(psB[:, :], ones_rf[:, :], bvv_sb[:, :],
                                      start=True, stop=True), 1)
                bc_pe = s_pe.n

                @block.vector
                def _(v):
                    v.wait_ge(s_pe.h, bc_pe)
                    v.tensor_copy(A_sb[:, :], psA[:, :])
                    v.tensor_copy(B_sb[:, :], psB[:, :])
                    last = None
                    for et in range(ET):
                        v.tensor_tensor(xnt_sb[:, :], xsl(et), A_sb[:, :], ALU.mult)
                        last = v.tensor_tensor(lsl(et), xnt_sb[:, :], B_sb[:, :], ALU.add)
                    done[qc] = s_dve.inc(last, 1)
            pse.close()
            return done

        def emit_ffn(tag, xln, xln_done, entry_pe_wait, entry_cc_g, entry_cc_u,
                     entry_cc_d):
            last_stt = None
            prev_qc_B_pe = None
            for qc in range(2):
                qoff = qc * QC
                fse = ExitStack()
                h_sb = fse.enter_context(nc.sbuf_tensor(f"h{tag}{qc}", [128, NIT * QC], F16))
                ase = ExitStack()
                wa_buf = [ase.enter_context(nc.sbuf_tensor(f"wa{tag}{qc}{i}", [128, ET * 128], BF16)) for i in range(2)]
                pse = ExitStack()
                psP = [pse.enter_context(nc.psum_tensor(f"psP{tag}{qc}{i}", [128, QC], F32)) for i in range(2)]
                psH = [pse.enter_context(nc.psum_tensor(f"psH{tag}{qc}{i}", [128, QC], F32)) for i in range(2)]

                passes = [("g", entry_cc_g), ("u", entry_cc_u)]
                ldA_sched, mmA_sched = {}, {}
                _pa = [d_pa[0].n, d_pa[1].n]
                k = 0
                for pname, _cc in passes:
                    for si in range(NIT):
                        _pa[k % 2] += 16
                        ldA_sched[(pname, si)] = _pa[k % 2]
                        k += 1
                npe = s_pe.n
                for pname, _cc in passes:
                    for si in range(NIT):
                        npe += 1
                        mmA_sched[(pname, si)] = npe
                silu_sched = {si: s_act.n + (si + 1) for si in range(NIT)}
                h_sched = {si: s_dve.n + (si + 1) for si in range(NIT)}
                pqB = prev_qc_B_pe

                @block.sync
                def _(sy):
                    prev_mm = [None, None]
                    k = 0
                    for pname, ccm in passes:
                        for si in range(NIT):
                            p = k % 2
                            if pname == "g" and si == 0:
                                if ccm is not None:
                                    sy.wait_ge(s_cc.h, ccm)
                                if qc == 0 and entry_pe_wait is not None:
                                    sy.wait_ge(s_pe.h, entry_pe_wait)
                                if qc == 1 and pqB is not None:
                                    sy.wait_ge(s_pe.h, pqB)
                            if pname == "u" and si == 0 and ccm is not None:
                                sy.wait_ge(s_cc.h, ccm)
                            if prev_mm[p] is not None:
                                sy.wait_ge(s_pe.h, prev_mm[p])
                            prev_mm[p] = mmA_sched[(pname, si)]
                            c0 = si * 128
                            mk = d_pa[p].inc(sy.dma_start(
                                out=wa_buf[p][:, :].rearrange("p (e c) -> p e c", c=128),
                                in_=ag_out[pname][:, c0:c0 + 128].rearrange("(e p) c -> p e c", p=128)), 16)
                            assert mk == ldA_sched[(pname, si)]
                            k += 1

                @block.tensor
                def _(t):
                    k = 0
                    for pname, _cc in passes:
                        for si in range(NIT):
                            p = k % 2
                            if pname == "g" and si == 0:
                                t.wait_ge(s_dve.h, xln_done[qc])
                            if pname == "u" and si == 0 and entry_cc_g is None:
                                t.wait_ge(s_dve.h, xln_done[1])
                            t.wait_ge(d_pa[p].h, ldA_sched[(pname, si)])
                            if pname == "g" and si >= 2:
                                t.wait_ge(s_act.h, silu_sched[si - 2])
                            if pname == "u" and si >= 2:
                                t.wait_ge(s_dve.h, h_sched[si - 2])
                            ps = psP if pname == "g" else psH
                            last = None
                            for et in range(ET):
                                last = t.matmul(ps[si % 2][:, :],
                                                wa_buf[p][:, et * 128:(et + 1) * 128],
                                                xln[:, et * Q + qoff: et * Q + qoff + QC],
                                                start=(et == 0), stop=(et == ET - 1))
                            assert s_pe.inc(last, 1) == mmA_sched[(pname, si)]
                            k += 1

                @block.scalar
                def _(sc):
                    for si in range(NIT):
                        sc.wait_ge(s_pe.h, mmA_sched[("g", si)])
                        mk = sc.activation(h_sb[:, si * QC:(si + 1) * QC], psP[si % 2][:, :],
                                           AF.Silu, bias=0.0, scale=1.0)
                        assert s_act.inc(mk, 1) == silu_sched[si]

                @block.vector
                def _(v):
                    for si in range(NIT):
                        v.wait_ge(s_pe.h, mmA_sched[("u", si)])
                        v.wait_ge(s_act.h, silu_sched[si])
                        mk = v.tensor_tensor(h_sb[:, si * QC:(si + 1) * QC],
                                             h_sb[:, si * QC:(si + 1) * QC],
                                             psH[si % 2][:, :], ALU.mult)
                        assert s_dve.inc(mk, 1) == h_sched[si]

                phaseA_pe = s_pe.n
                h_all = s_dve.n
                ase.close()
                pse.close()

                bse = ExitStack()
                wd_buf = [bse.enter_context(nc.sbuf_tensor(f"wd{tag}{qc}{i}", [128, 4 * 1024], BF16)) for i in range(2)]
                ps2 = ExitStack()
                psY = [ps2.enter_context(nc.psum_tensor(f"psY{tag}{qc}{i}", [128, QC], F32)) for i in range(8)]

                groups = []
                it0 = 0
                while it0 < NIT:
                    groups.append((it0, min(4, NIT - it0)))
                    it0 += 4
                NG = len(groups)
                bsteps = [(hf, gi) for hf in range(2) for gi in range(NG)]
                ldB_sched = {}
                _pb = [d_pa[0].n, d_pa[1].n]
                for (hf, gi) in bsteps:
                    p = gi % 2
                    _pb[p] += 16
                    ldB_sched[(hf, gi)] = _pb[p]
                mmB_sched = {s: s_pe.n + (i + 1) for i, s in enumerate(bsteps)}
                stt_sched = {hf: s_dve.n + 8 * (hf + 1) for hf in range(2)}

                @block.sync
                def _(sy):
                    last_par = {0: None, 1: None}
                    for i, (hf, gi) in enumerate(bsteps):
                        it0, ng = groups[gi]
                        if gi == 0:
                            if entry_cc_d is not None:
                                sy.wait_ge(s_cc.h, entry_cc_d[hf])
                            if i == 0:
                                sy.wait_ge(s_dve.h, h_all)
                        p = gi % 2
                        if last_par[p] is not None:
                            sy.wait_ge(s_pe.h, mmB_sched[bsteps[last_par[p]]])
                        last_par[p] = i
                        mk = d_pa[p].inc(sy.dma_start(
                            out=wd_buf[gi % 2][:, 0:ng * 1024].rearrange("p (t c) -> p t c", c=1024),
                            in_=ag_out[f"d{hf}"][it0 * 128:(it0 + ng) * 128,
                                                 :].rearrange("(t p) c -> p t c", p=128)), 16)
                        assert mk == ldB_sched[(hf, gi)]

                @block.tensor
                def _(t):
                    for (hf, gi) in bsteps:
                        it0, ng = groups[gi]
                        if gi == 0:
                            t.wait_ge(s_dve.h, h_all)
                            if hf == 1:
                                t.wait_ge(s_dve.h, stt_sched[0])
                        t.wait_ge(d_pa[gi % 2].h, ldB_sched[(hf, gi)])
                        last = None
                        for il in range(ng):
                            for eo in range(8):
                                last = t.matmul(psY[eo][:, :],
                                                wd_buf[gi % 2][:, il * 1024 + eo * 128: il * 1024 + (eo + 1) * 128],
                                                h_sb[:, (it0 + il) * QC:(it0 + il + 1) * QC],
                                                start=(it0 + il == 0), stop=(it0 + il == NIT - 1))
                        assert s_pe.inc(last, 1) == mmB_sched[(hf, gi)]

                @block.vector
                def _(v):
                    for hf in range(2):
                        v.wait_ge(s_pe.h, mmB_sched[(hf, NG - 1)])
                        mk = None
                        for eo in range(8):
                            eog = hf * 8 + eo
                            xsl = xT[:, eog * Q + qoff: eog * Q + qoff + QC]
                            mk = v.scalar_tensor_tensor(xsl, psY[eo][:, :], 0.5, xsl,
                                                        ALU.mult, ALU.add)
                        assert s_dve.inc(mk, 8) == stt_sched[hf]

                last_stt = s_dve.n
                prev_qc_B_pe = s_pe.n
                ps2.close()
                bse.close()
                fse.close()
            return last_stt, s_pe.n

        _sc_mat = nc.named_scope("ph_mat"); _sc_mat.__enter__()
        ln_es1 = ExitStack()
        xln1 = ln_es1.enter_context(nc.sbuf_tensor("xln1", [128, ET * Q], BF16))
        mat_es = ExitStack()
        NIQ = NI // 4
        midx = [mat_es.enter_context(nc.sbuf_tensor(f"midx{i}", [128, NIQ], I32))
                for i in range(2)]
        midx16 = mat_es.enter_context(nc.sbuf_tensor("midx16", [128, NIQ], F16))
        mt_sb = mat_es.enter_context(nc.sbuf_tensor("mt_sb", [128, NIQ], BF16))
        macc = mat_es.enter_context(nc.sbuf_tensor("macc", [128, NIQ], BF16))

        idx_srcs = {"g": gidx_in, "u": uidx_in, "d0": didx_in, "d1": didx_in}

        _prev_mat_done = [None, 0]

        def emit_mat_part(tiles, ag_key):
            NT = len(tiles)
            mat_ld = {}
            _ml = [d_mg[0].n, d_mg[1].n]
            mat_par = {}
            pi0 = _prev_mat_done[1]
            for i in range(NT):
                p = (pi0 + i) % 2
                _ml[p] += 16
                mat_ld[i] = _ml[p]
                mat_par[i] = p
            mat_done = {i: s_dve.n + (i + 1) for i in range(NT)}
            mat_st = {i: d_gst.n + 16 * (i + 1) for i in range(NT)}
            pd = _prev_mat_done[0]

            @block.gpsimd
            def _(gp):
                for i, (mm_, r0, pc, c0, dc0, w) in enumerate(tiles):
                    prev = mat_done[i - 2] if i > 1 else (pd if i > 0 else None)
                    if i == 0 and pd is not None:
                        pass
                    if prev is not None:
                        gp.wait_ge(s_dve.h, prev)
                    mk = d_mg[mat_par[i]].inc(gp.dma_start(
                        out=midx[mat_par[i]][0:pc, 0:w],
                        in_=idx_srcs[mm_][r0:r0 + pc, c0:c0 + w]), 16)
                    assert mk == mat_ld[i]
                    if i > 0:
                        gp.wait_ge(s_dve.h, mat_done[i - 1])
                        pm = tiles[i - 1]
                        mk = d_gst.inc(gp.dma_start(
                            out=ag_in[pm[0]][pm[1]:pm[1] + pm[2], pm[4]:pm[4] + pm[5]],
                            in_=macc[0:pm[2], 0:pm[5]]), 16)
                        assert mk == mat_st[i - 1]
                lm = tiles[NT - 1]
                gp.wait_ge(s_dve.h, mat_done[NT - 1])
                mk = d_gst.inc(gp.dma_start(
                    out=ag_in[lm[0]][lm[1]:lm[1] + lm[2], lm[4]:lm[4] + lm[5]],
                    in_=macc[0:lm[2], 0:lm[5]]), 16)
                assert mk == mat_st[NT - 1]
                gp.wait_ge(d_gst.h, mat_st[NT - 1])
                cc_w[ag_key] = s_cc.inc(gp.collective_compute(
                    "AllGather", ALU.bypass, replica_groups=[list(range(NCORE))],
                    ins=[ag_in[ag_key].ap().opt()], outs=[ag_out[ag_key].ap().opt()]), 1)

            @block.vector
            def _(v):
                for i, (mm_, r0, pc, c0, dc0, w) in enumerate(tiles):
                    v.wait_ge(d_mg[mat_par[i]].h, mat_ld[i])
                    if i > 0:
                        v.wait_ge(d_gst.h, mat_st[i - 1])
                    v.tensor_copy(midx16[0:pc, 0:w], midx[mat_par[i]][0:pc, 0:w])
                    v.tensor_scalar(macc[0:pc, 0:w], midx16[0:pc, 0:w], 0.0,
                                    lut_sb[0:pc, 0:1], ALU.is_equal, ALU.mult)
                    last = None
                    for k in range(1, 16):
                        v.tensor_scalar(mt_sb[0:pc, 0:w], midx16[0:pc, 0:w], float(k),
                                        lut_sb[0:pc, k:k + 1], ALU.is_equal, ALU.mult)
                        last = v.tensor_tensor(macc[0:pc, 0:w], macc[0:pc, 0:w],
                                               mt_sb[0:pc, 0:w], ALU.add)
                    assert s_dve.inc(last, 1) == mat_done[i]

            _prev_mat_done[0] = mat_done[NT - 1]
            _prev_mat_done[1] = (pi0 + NT) % 2
            return mat_st[NT - 1]

        cc_w = {}

        def _gu_tiles(m):
            return [(m, r0, 128, c0, c0, NIQ) for r0 in (0, 128)
                    for c0 in (0, NIQ, 2 * NIQ, 3 * NIQ)]

        def _d_tiles(half):
            return [(f"d{half}", r0, min(128, 688 - r0), half * EH + c0, c0, EH // 2)
                    for r0 in range(0, 688, 128) for c0 in (0, EH // 2)]

        g_st = emit_mat_part(_gu_tiles("g"), "g")
        with nc.named_scope("ph_ln1"):
            ln1_done = emit_ln("l1a", xln1, x_ready_dma=x_ready, qcs=(0,))
        u_st = emit_mat_part(_gu_tiles("u"), "u")
        with nc.named_scope("ph_ln1"):
            res_ps = ExitStack()
            for _i in range(4):
                res_ps.enter_context(nc.psum_tensor(f"resv{_i}", [128, QC], F32))
            ln1_done.update(emit_ln("l1b", xln1, x_ready_dma=x_ready, qcs=(1,)))
            res_ps.close()
        d0_st = emit_mat_part(_d_tiles(0), "d0")
        d1_st = emit_mat_part(_d_tiles(1), "d1")

        _sc_mat.__exit__(None, None, None)

        if DEBUG:
            @block.scalar
            def _(sc):
                sc.wait_ge(s_dve.h, ln1_done[1])
                d_scal.inc(sc.dma_start(out=taps["xln1"][:, :], in_=xln1[:, :]), 16)

            @block.gpsimd
            def _(gp):
                gp.wait_ge(s_cc.h, cc_w["d1"])
                d_gp.inc(gp.dma_start(out=taps["wg"][:, :], in_=ag_out["g"][:, :]), 16)
                d_gp.inc(gp.dma_start(out=taps["wd"][:, 0:EH], in_=ag_out["d0"][:, :]), 16)
                d_gp.inc(gp.dma_start(out=taps["wd"][:, EH:E], in_=ag_out["d1"][:, :]), 16)
        with nc.named_scope("ph_ffn1"):
            ffn1_stt, ffn1_pe = emit_ffn("f1", xln1, ln1_done, None,
                                         cc_w["g"], cc_w["u"],
                                         (cc_w["d0"], cc_w["d1"]))
        mat_es.close()
        ln_es1.close()
        if DEBUG:
            @block.scalar
            def _(sc):
                sc.wait_ge(s_dve.h, ffn1_stt)
                d_scal.inc(sc.dma_start(out=taps["x1"][:, :], in_=xT[:, :]), 16)

        ln_es2 = ExitStack()
        xln2 = ln_es2.enter_context(nc.sbuf_tensor("xln2", [128, ET * Q], BF16))
        with nc.named_scope("ph_ln2"):
            ln2_done = emit_ln("l2", xln2, x_ready_dve=ffn1_stt)

        _sc_qkv = nc.named_scope("ph_qkv"); _sc_qkv.__enter__()
        qkv_es = ExitStack()
        wq_buf = [qkv_es.enter_context(nc.sbuf_tensor(f"wqb{i}", [128, ET * 128], BF16)) for i in range(2)]
        qt_tmp = [qkv_es.enter_context(nc.sbuf_tensor(f"qt{i}", [128, Q], BF16)) for i in range(2)]
        rt_tmp = qkv_es.enter_context(nc.sbuf_tensor("rt", [128, Q], BF16))
        qkv_ps = ExitStack()
        psQ = [qkv_ps.enter_context(nc.psum_tensor(f"psQ{i}", [128, QC], F32)) for i in range(2)]
        psSW = [qkv_ps.enter_context(nc.psum_tensor(f"psSW{i}", [128, Q], F32)) for i in range(2)]
        qkv_entry_pe = ffn1_pe

        projs = [("k", wkT, 0), ("v", wvT, 256), ("q", wqT, 0)]
        NPF = 3 * ET
        qmm_sched, swmm_sched = {}, {}
        npe = s_pe.n
        for i in range(NPF):
            pname = projs[i // ET][0]
            npe += 1
            qmm_sched[i] = npe
            if pname in ("q", "k"):
                npe += 1
                swmm_sched[i] = npe
        qev_sched, qrp_sched, qld_sched, qst_sched = {}, {}, {}, {}
        nd = s_dve.n
        _pq = [d_pa[0].n, d_pa[1].n]
        _sq = [d_stq[0].n, d_stq[1].n]
        for i in range(NPF):
            pname = projs[i // ET][0]
            _pq[i % 2] += 16
            qld_sched[i] = _pq[i % 2]
            nd += 1
            qev_sched[i] = nd
            if pname in ("q", "k"):
                nd += 1
                qrp_sched[i] = nd
            _sq[i % 2] += 16
            qst_sched[i] = _sq[i % 2]

        @block.gpsimd
        def _(gp):
            for i in range(NPF):
                pi, ft = divmod(i, ET)
                pname, wt, _off = projs[pi]
                if i == 0:
                    gp.wait_ge(s_pe.h, qkv_entry_pe)
                if i >= 2:
                    gp.wait_ge(s_pe.h, qmm_sched[i - 2])
                src_ap = wt[:, ft * 128:(ft + 1) * 128].rearrange("(t p) c -> p t c", p=128)
                mk = d_pa[i % 2].inc(gp.dma_start(
                    out=wq_buf[i % 2][:, :].rearrange("p (e c) -> p e c", c=128),
                    in_=src_ap), 16)
                assert mk == qld_sched[i]

        @block.tensor
        def _(t):
            for i in range(NPF):
                pi, ft = divmod(i, ET)
                pname = projs[pi][0]
                if i == 0:
                    t.wait_ge(s_dve.h, ln2_done[1])
                t.wait_ge(d_pa[i % 2].h, qld_sched[i])
                if i >= 1:
                    t.wait_ge(s_dve.h, qev_sched[i - 1])
                if i >= 2:
                    t.wait_ge(d_stq[i % 2].h, qst_sched[i - 2])
                last = None
                for et in range(ET):
                    for qc in range(2):
                        last = t.matmul(psQ[qc][:, :],
                                        wq_buf[i % 2][:, et * 128:(et + 1) * 128],
                                        xln2[:, et * Q + qc * QC: et * Q + qc * QC + QC],
                                        start=(et == 0), stop=(et == ET - 1))
                assert s_pe.inc(last, 1) == qmm_sched[i]
                if pname in ("q", "k"):
                    t.wait_ge(s_dve.h, qev_sched[i])
                    pr = [j for j in range(i - 1, -1, -1)
                          if j % 2 == i % 2 and j in qrp_sched]
                    if pr:
                        t.wait_ge(s_dve.h, qrp_sched[pr[0]])
                    t.matmul(psSW[i % 2][:, 0:QC], rot_sb[:, :], qt_tmp[i % 2][:, 0:QC],
                             start=True, stop=True)
                    mk = t.matmul(psSW[i % 2][:, QC:Q], rot_sb[:, :], qt_tmp[i % 2][:, QC:Q],
                                  start=True, stop=True)
                    assert s_pe.inc(mk, 1) == swmm_sched[i]

        @block.vector
        def _(v):
            for i in range(NPF):
                pi, ft = divmod(i, ET)
                pname = projs[pi][0]
                v.wait_ge(s_pe.h, qmm_sched[i])
                v.tensor_scalar(qt_tmp[i % 2][:, 0:QC], psQ[0][:, :],
                                bias_sb[pname][:, ft:ft + 1], None, ALU.add)
                mk = v.tensor_scalar(qt_tmp[i % 2][:, QC:Q], psQ[1][:, :],
                                     bias_sb[pname][:, ft:ft + 1], None, ALU.add)
                assert s_dve.inc(mk, 1) == qev_sched[i]
                if pname in ("q", "k"):
                    v.wait_ge(s_pe.h, swmm_sched[i])
                    v.tensor_tensor(rt_tmp[:, :], psSW[i % 2][:, :], sin_sb[:, :], ALU.mult)
                    v.tensor_tensor(qt_tmp[i % 2][:, :], qt_tmp[i % 2][:, :], cos_sb[:, :], ALU.mult)
                    mk = v.tensor_tensor(qt_tmp[i % 2][:, :], qt_tmp[i % 2][:, :],
                                         rt_tmp[:, :], ALU.add)
                    assert s_dve.inc(mk, 1) == qrp_sched[i]

        @block.scalar
        def _(sc):
            for i in range(NPF):
                pi, ft = divmod(i, ET)
                pname, _w, off = projs[pi]
                sc.wait_ge(s_dve.h, qrp_sched.get(i, qev_sched[i]))
                if pname == "q":
                    r0 = (ft // 2) * 256 + (ft % 2) * 128
                    dst = agq_in
                else:
                    r0 = (ft // 2) * 512 + off + (ft % 2) * 128
                    dst = agkv_in
                mk = d_stq[i % 2].inc(sc.dma_start(
                    out=dst[r0:r0 + 128, :],
                    in_=qt_tmp[i % 2][:, :]), 16)
                assert mk == qst_sched[i]

        kv_last = {p: max(qst_sched[i] for i in range(2 * ET) if i % 2 == p)
                   for p in range(2)}

        @block.gpsimd
        def _(gp):
            gp.wait_ge(d_stq[0].h, kv_last[0])
            gp.wait_ge(d_stq[1].h, kv_last[1])
            with allow_ici_all_to_all():
                s_cc.inc(gp.collective_compute(
                    "AllToAll", ALU.bypass, replica_groups=[list(range(NCORE))],
                    ins=[agkv_in.ap().opt()], outs=[agkv_out.ap().opt()]), 1)
            gp.wait_ge(d_stq[0].h, d_stq[0].n)
            gp.wait_ge(d_stq[1].h, d_stq[1].n)
            with allow_ici_all_to_all():
                s_cc.inc(gp.collective_compute(
                    "AllToAll", ALU.bypass, replica_groups=[list(range(NCORE))],
                    ins=[agq_in.ap().opt()], outs=[agq_out.ap().opt()]), 1)

        if DEBUG:
            @block.gpsimd
            def _(gp):
                gp.wait_ge(s_cc.h, s_cc.n)
                d_gp.inc(gp.dma_start(out=taps["qkv"][:, :], in_=agq_in[:, :]), 16)
        cc_q = s_cc.n
        cc_kv = cc_q - 1
        qkv_ps.close()
        qkv_es.close()
        ln_es2.close()
        _sc_qkv.__exit__(None, None, None)

        _sc_at = nc.named_scope("ph_attn"); _sc_at.__enter__()
        at_es = ExitStack()
        KT_sb = at_es.enter_context(nc.sbuf_tensor("KT_sb", [128, 2 * S], BF16))
        V_sb = at_es.enter_context(nc.sbuf_tensor("V_sb", [128, 2 * S], BF16))
        VT_tmp = [at_es.enter_context(nc.sbuf_tensor(f"VT{i}", [128, Q], BF16)) for i in range(2)]
        QT_sb = [at_es.enter_context(nc.sbuf_tensor(f"QT{i}", [128, QC], BF16)) for i in range(2)]
        PT_sb = [at_es.enter_context(nc.sbuf_tensor(f"PT{i}", [128, QC], BF16)) for i in range(4)]
        at_ps = ExitStack()
        psS = [at_ps.enter_context(nc.psum_tensor(f"psS{i}", [128, QC], F32)) for i in range(4)]
        psO = [at_ps.enter_context(nc.psum_tensor(f"psO{i}", [128, QC], F32)) for i in range(2)]
        psR = at_ps.enter_context(nc.psum_tensor("psR", [128, QC], F32))
        psT = at_ps.enter_context(nc.psum_tensor("psT", [128, Q], BF16))

        kvt = [(hl, c) for hl in range(2) for c in range(NCORE)]
        kv_ld_sched = {t: d_gp.n + 16 * (i + 1) for i, t in enumerate(kvt)}
        vt_ld_sched = {}
        _pv = [d_pa[0].n, d_pa[1].n]
        for i, t in enumerate(kvt):
            _pv[i % 2] += 16
            vt_ld_sched[t] = _pv[i % 2]
        vt_par = {t: (i % 2) for i, t in enumerate(kvt)}
        vtr_sched = {t: s_pe.n + (i + 1) for i, t in enumerate(kvt)}
        vev_sched = {t: s_dve.n + (i + 1) for i, t in enumerate(kvt)}

        @block.gpsimd
        def _(gp):
            gp.wait_ge(s_cc.h, cc_kv)
            for (hl, c) in kvt:
                r0 = c * 512 + hl * 128
                mk = d_gp.inc(gp.dma_start(
                    out=KT_sb[:, hl * S + c * Q: hl * S + (c + 1) * Q],
                    in_=agkv_out[r0:r0 + 128, :]), 16)
                assert mk == kv_ld_sched[(hl, c)]
            for i, (hl, c) in enumerate(kvt):
                if i >= 2:
                    gp.wait_ge(s_pe.h, vtr_sched[kvt[i - 2]])
                r0 = c * 512 + 256 + hl * 128
                mk = d_pa[i % 2].inc(gp.dma_start(
                    out=VT_tmp[i % 2][:, :],
                    in_=agkv_out[r0:r0 + 128, :]), 16)
                assert mk == vt_ld_sched[(hl, c)]

        @block.tensor
        def _(t):
            for i, (hl, c) in enumerate(kvt):
                t.wait_ge(d_pa[vt_par[(hl, c)]].h, vt_ld_sched[(hl, c)])
                if i >= 1:
                    t.wait_ge(s_dve.h, vev_sched[kvt[i - 1]])
                last = None
                for b in range(8):
                    last = t.transpose(psT[:, b * 128:(b + 1) * 128],
                                       VT_tmp[i % 2][:, b * 128:(b + 1) * 128],
                                       ident[:, :])
                assert s_pe.inc(last, 1) == vtr_sched[(hl, c)]

        @block.vector
        def _(v):
            for (hl, c) in kvt:
                v.wait_ge(s_pe.h, vtr_sched[(hl, c)])
                mk = v.tensor_copy(V_sb[:, hl * S + c * Q: hl * S + (c + 1) * Q], psT[:, :])
                assert s_dve.inc(mk, 1) == vev_sched[(hl, c)]

        chunks = [(hl, qk) for hl in range(2) for qk in range(NQK)]
        _on_gp = lambda kb: (kb % 5) in (1, 3)
        DEP = 4
        gb, gidx = [], {}
        for ci, (hl, qk) in enumerate(chunks):
            for kb in range(4 * qk + 4):
                gidx[(hl, qk, kb)] = len(gb)
                gb.append((ci, hl, qk, kb))
        NG = len(gb)
        qt_ld_sched, qt_par = {}, {}
        mmS_sched, exp_sched, pvl_sched, mask_sched = {}, {}, {}, {}
        l_sched, l_last, lsum_sched = {}, {}, {}
        rec_sched, oev_sched, ost_sched = {}, {}, {}
        npe, nact, ndv, ngp, nsc = s_pe.n, s_act.n, s_dve.n, s_gp.n, d_scal.n
        _pt = [d_pa[0].n, d_pa[1].n]
        for ci, (hl, qk) in enumerate(chunks):
            _pt[ci % 2] += 16
            qt_ld_sched[(hl, qk)] = _pt[ci % 2]
            qt_par[(hl, qk)] = ci % 2
        for gi in range(NG + DEP - 1):
            if gi < NG:
                ci, hl, qk, kb = gb[gi]
                npe += 1
                mmS_sched[(hl, qk, kb)] = npe
            pj = gi - (DEP - 1)
            if pj >= 0:
                pci, phl, pqk, pkb = gb[pj]
                npe += 1
                pvl_sched[(phl, pqk, pkb)] = npe
                if pkb == 4 * pqk + 3:
                    npe += 2
                    lsum_sched[(phl, pqk)] = npe
        for ci, (hl, qk) in enumerate(chunks):
            nb = 4 * qk + 4
            l_last[(hl, qk)] = {e: max(k for k in range(nb)
                                       if (_on_gp(k) == (e == "g"))) for e in "dg"}
            for kb in range(nb):
                nact += 1
                exp_sched[(hl, qk, kb)] = nact
                if kb >= 4 * qk:
                    ndv += 1
                    mask_sched[(hl, qk, kb)] = ndv
                if _on_gp(kb):
                    ngp += 1
                    l_sched[(hl, qk, kb)] = ("g", ngp)
                else:
                    ndv += 1
                    l_sched[(hl, qk, kb)] = ("d", ndv)
                if kb == 2 and ci >= 1:
                    ndv += 1
                    rec_sched[chunks[ci - 1]] = ndv
                    ndv += 1
                    oev_sched[chunks[ci - 1]] = ndv
            if ci >= 1:
                nsc += 16
                ost_sched[chunks[ci - 1]] = nsc
        ndv += 1
        rec_sched[chunks[-1]] = ndv
        ndv += 1
        oev_sched[chunks[-1]] = ndv
        nsc += 16
        ost_sched[chunks[-1]] = nsc
        l_sem = {"d": s_dve, "g": s_gp}

        def _emit_qt(gp, cj):
            hl2, qk2 = chunks[cj]
            r0 = (qk2 // 2) * 256 + hl2 * 128
            mk = d_pa[cj % 2].inc(gp.dma_start(
                out=QT_sb[cj % 2][:, :],
                in_=agq_out[r0:r0 + 128, (qk2 % 2) * QC:(qk2 % 2) * QC + QC]), 16)
            assert mk == qt_ld_sched[(hl2, qk2)]

        @block.gpsimd
        def _(gp):
            gp.wait_ge(s_pe.h, vtr_sched[kvt[-1]])
            gp.wait_ge(s_cc.h, cc_q)
            _emit_qt(gp, 0)
            _emit_qt(gp, 1)
            def _emit_ost(cj):
                hl2, qk2 = chunks[cj]
                gp.wait_ge(s_dve.h, oev_sched[(hl2, qk2)])
                r0 = (qk2 // 2) * 256 + hl2 * 128
                mk = gp.dma_start(
                    out=ago_in[r0:r0 + 128, (qk2 % 2) * QC:(qk2 % 2) * QC + QC],
                    in_=o_sb[:, :])
                assert d_scal.inc(mk, 16) == ost_sched[(hl2, qk2)]

            for ci, (hl, qk) in enumerate(chunks):
                nb = 4 * qk + 4
                first = True
                for kb in range(nb):
                    if not _on_gp(kb):
                        continue
                    gi = gidx[(hl, qk, kb)]
                    j = kb - 4 * qk
                    u0 = 128 * j if j >= 0 else 0
                    gp.wait_ge(s_act.h, exp_sched[(hl, qk, kb)])
                    if first:
                        if ci >= 1:
                            gp.wait_ge(s_pe.h, lsum_sched[chunks[ci - 1]])
                        if u0 > 0:
                            gp.memset(lb_sb[:, 0:u0], 0.0)
                        mk = gp.tensor_copy(lb_sb[:, u0:QC], PT_sb[gi % DEP][:, u0:QC])
                        first = False
                    else:
                        mk = gp.tensor_tensor(lb_sb[:, u0:QC], lb_sb[:, u0:QC],
                                              PT_sb[gi % DEP][:, u0:QC], ALU.add)
                    assert s_gp.inc(mk, 1) == l_sched[(hl, qk, kb)][1]
                if ci + 2 < len(chunks):
                    gp.wait_ge(s_pe.h, mmS_sched[(hl, qk, 4 * qk + 3)])
                    _emit_qt(gp, ci + 2)
                if ci >= 1:
                    _emit_ost(ci - 1)
            _emit_ost(len(chunks) - 1)

        @block.tensor
        def _(t):
            for gi in range(NG + DEP - 1):
                if gi < NG:
                    ci, hl, qk, kb = gb[gi]
                    j = kb - 4 * qk
                    u0 = 128 * j if j >= 0 else 0
                    if kb == 0:
                        t.wait_ge(d_pa[qt_par[(hl, qk)]].h, qt_ld_sched[(hl, qk)])
                        if ci == 0:
                            t.wait_ge(d_gp.h, kv_ld_sched[kvt[-1]])
                    if gi >= DEP:
                        _, phl, pqk, pkb = gb[gi - DEP]
                        t.wait_ge(s_act.h, exp_sched[(phl, pqk, pkb)])
                    mk = t.matmul(psS[gi % DEP][:, u0:QC],
                                  KT_sb[:, hl * S + kb * 128: hl * S + (kb + 1) * 128],
                                  QT_sb[ci % 2][:, u0:QC], start=True, stop=True)
                    assert s_pe.inc(mk, 1) == mmS_sched[(hl, qk, kb)]
                pj = gi - (DEP - 1)
                if pj < 0:
                    continue
                pci, phl, pqk, pkb = gb[pj]
                pnb = 4 * pqk + 4
                pj0 = 128 * (pkb - 4 * pqk) if pkb >= 4 * pqk else 0
                t.wait_ge(s_act.h, exp_sched[(phl, pqk, pkb)])
                if pkb == 0 and pci >= 2:
                    t.wait_ge(s_dve.h, oev_sched[chunks[pci - 2]])
                mk = t.matmul(psO[pci % 2][:, pj0:QC],
                              V_sb[:, phl * S + pkb * 128: phl * S + (pkb + 1) * 128],
                              PT_sb[pj % DEP][:, pj0:QC],
                              start=(pkb == 0), stop=(pkb == pnb - 1))
                assert s_pe.inc(mk, 1) == pvl_sched[(phl, pqk, pkb)]
                if pkb == pnb - 1:
                    t.wait_ge(s_dve.h, l_sched[(phl, pqk, l_last[(phl, pqk)]["d"])][1])
                    t.wait_ge(s_gp.h, l_sched[(phl, pqk, l_last[(phl, pqk)]["g"])][1])
                    if pci >= 1:
                        t.wait_ge(s_dve.h, rec_sched[chunks[pci - 1]])
                    t.matmul(psR[:, :], ones_ff[:, :], la_sb[:, :], start=True, stop=False)
                    mk = t.matmul(psR[:, :], ones_ff[:, :], lb_sb[:, :], start=False, stop=True)
                    assert s_pe.inc(mk, 2) == lsum_sched[(phl, pqk)]

        @block.vector
        def _(v):
            def _rec_oev(cj):
                hl2, qk2 = chunks[cj]
                v.wait_ge(s_pe.h, lsum_sched[(hl2, qk2)])
                mk = v.reciprocal(R_sb[:, :], psR[:, :])
                assert s_dve.inc(mk, 1) == rec_sched[(hl2, qk2)]
                if cj >= 1:
                    v.wait_ge(d_scal.h, ost_sched[chunks[cj - 1]])
                mk = v.tensor_tensor(o_sb[:, :], psO[cj % 2][:, :], R_sb[:, :], ALU.mult)
                assert s_dve.inc(mk, 1) == oev_sched[(hl2, qk2)]

            for ci, (hl, qk) in enumerate(chunks):
                nb = 4 * qk + 4
                first = True
                for kb in range(nb):
                    gi = gidx[(hl, qk, kb)]
                    if kb >= 4 * qk:
                        u0 = 128 * (kb - 4 * qk)
                        v.wait_ge(s_pe.h, mmS_sched[(hl, qk, kb)])
                        mk = v.tensor_tensor(psS[gi % DEP][:, u0:u0 + 128],
                                             psS[gi % DEP][:, u0:u0 + 128], tri_sb[:, :], ALU.add)
                        assert s_dve.inc(mk, 1) == mask_sched[(hl, qk, kb)]
                    if not _on_gp(kb):
                        j = kb - 4 * qk
                        u0 = 128 * j if j >= 0 else 0
                        v.wait_ge(s_act.h, exp_sched[(hl, qk, kb)])
                        if first:
                            if ci >= 1:
                                v.wait_ge(s_pe.h, lsum_sched[chunks[ci - 1]])
                            if u0 > 0:
                                v.memset(la_sb[:, 0:u0], 0.0)
                            mk = v.tensor_copy(la_sb[:, u0:QC], PT_sb[gi % DEP][:, u0:QC])
                            first = False
                        else:
                            mk = v.tensor_tensor(la_sb[:, u0:QC], la_sb[:, u0:QC],
                                                 PT_sb[gi % DEP][:, u0:QC], ALU.add)
                        assert s_dve.inc(mk, 1) == l_sched[(hl, qk, kb)][1]
                    if kb == 2 and ci >= 1:
                        _rec_oev(ci - 1)
            _rec_oev(len(chunks) - 1)

        @block.scalar
        def _(sc):
            for gi in range(NG):
                ci, hl, qk, kb = gb[gi]
                j = kb - 4 * qk
                u0 = 128 * j if j >= 0 else 0
                sc.wait_ge(s_pe.h, mmS_sched[(hl, qk, kb)])
                if (hl, qk, kb) in mask_sched:
                    sc.wait_ge(s_dve.h, mask_sched[(hl, qk, kb)])
                if gi >= DEP:
                    _, phl, pqk, pkb = gb[gi - DEP]
                    sc.wait_ge(s_pe.h, pvl_sched[(phl, pqk, pkb)])
                    e, cnt = l_sched[(phl, pqk, pkb)]
                    sc.wait_ge(l_sem[e].h, cnt)
                mk = sc.activation(PT_sb[gi % DEP][:, u0:QC], psS[gi % DEP][:, u0:QC],
                                   AF.Exp, bias=expb_sb[:, :], scale=SCALE)
                assert s_act.inc(mk, 1) == exp_sched[(hl, qk, kb)]

        if DEBUG:
            @block.scalar
            def _(sc):
                sc.wait_ge(s_dve.h, vev_sched[kvt[-1]])
                d_scal.inc(sc.dma_start(out=taps["kt"][:, :], in_=KT_sb[:, :]), 16)
                d_scal.inc(sc.dma_start(out=taps["vt"][:, :], in_=V_sb[:, :]), 16)

        @block.gpsimd
        def _(gp):
            gp.wait_ge(d_scal.h, ost_sched[chunks[-1]])
            with allow_ici_all_to_all():
                s_cc.inc(gp.collective_compute(
                    "AllToAll", ALU.bypass, replica_groups=[list(range(NCORE))],
                    ins=[ago_in.ap().opt()], outs=[ago_out.ap().opt()]), 1)

        cc_o = s_cc.n
        at_ps.close()
        at_es.close()
        _sc_at.__exit__(None, None, None)

        if DEBUG:
            @block.scalar
            def _(sc):
                sc.wait_ge(s_cc.h, cc_o)
                d_scal.inc(sc.dma_start(out=taps["x1b"][:, :], in_=xT[:, :]), 16)

        _sc_op = nc.named_scope("ph_outproj"); _sc_op.__enter__()
        op_es = ExitStack()
        OR_sb = op_es.enter_context(nc.sbuf_tensor("OR_sb", [128, ET * Q], BF16))
        wo_buf = [op_es.enter_context(nc.sbuf_tensor(f"wob{i}", [128, ET * 128], BF16)) for i in range(2)]
        op_ps = ExitStack()
        psY2 = [[op_ps.enter_context(nc.psum_tensor(f"psY2{i}{qc}", [128, QC], F32))
                 for qc in range(2)] for i in range(2)]

        or_ld_sched = {ft: d_gp.n + 16 * (ft + 1) for ft in range(ET)}
        wo_ld_sched = {}
        _pw = [d_pa[0].n, d_pa[1].n]
        for eo in range(ET):
            _pw[eo % 2] += 16
            wo_ld_sched[eo] = _pw[eo % 2]
        mmO_sched = {eo: s_pe.n + eo + 1 for eo in range(ET)}
        sttO_sched = {eo: s_dve.n + 2 * (eo + 1) for eo in range(ET)}

        @block.gpsimd
        def _(gp):
            gp.wait_ge(s_cc.h, cc_o)
            for ft in range(ET):
                mk = d_gp.inc(gp.dma_start(
                    out=OR_sb[:, ft * Q:(ft + 1) * Q],
                    in_=ago_out[ft * 128:(ft + 1) * 128, :]), 16)
                assert mk == or_ld_sched[ft]
            for eo in range(ET):
                if eo >= 2:
                    gp.wait_ge(s_pe.h, mmO_sched[eo - 2])
                mk = d_pa[eo % 2].inc(gp.dma_start(
                    out=wo_buf[eo % 2][:, :].rearrange("p (e c) -> p e c", c=128),
                    in_=woT[:, eo * 128:(eo + 1) * 128].rearrange("(t p) c -> p t c", p=128)), 16)
                assert mk == wo_ld_sched[eo]

        @block.tensor
        def _(t):
            for eo in range(ET):
                t.wait_ge(d_pa[eo % 2].h, wo_ld_sched[eo])
                if eo == 0:
                    t.wait_ge(d_gp.h, or_ld_sched[ET - 1])
                if eo >= 2:
                    t.wait_ge(s_dve.h, sttO_sched[eo - 2])
                last = None
                for qc in range(2):
                    for ft in range(ET):
                        last = t.matmul(psY2[eo % 2][qc][:, :],
                                        wo_buf[eo % 2][:, ft * 128:(ft + 1) * 128],
                                        OR_sb[:, ft * Q + qc * QC: ft * Q + qc * QC + QC],
                                        start=(ft == 0), stop=(ft == ET - 1))
                assert s_pe.inc(last, 1) == mmO_sched[eo]

        @block.vector
        def _(v):
            for eo in range(ET):
                v.wait_ge(s_pe.h, mmO_sched[eo])
                mk = None
                for qc in range(2):
                    xsl = xT[:, eo * Q + qc * QC: eo * Q + qc * QC + QC]
                    mk = v.scalar_tensor_tensor(xsl, psY2[eo % 2][qc][:, :],
                                                bias_sb["o"][:, eo:eo + 1], xsl,
                                                ALU.add, ALU.add)
                assert s_dve.inc(mk, 2) == sttO_sched[eo]

        outproj_pe = s_pe.n
        outproj_stt = s_dve.n
        if DEBUG:
            @block.scalar
            def _(sc):
                sc.wait_ge(s_dve.h, outproj_stt)
                d_scal.inc(sc.dma_start(out=taps["x2"][:, :], in_=xT[:, :]), 16)

            @block.gpsimd
            def _(gp):
                gp.wait_ge(s_cc.h, s_cc.n)
                d_gp.inc(gp.dma_start(out=taps["orr"][:, :], in_=ago_out[:, :]), 16)
        op_ps.close()
        op_es.close()
        _sc_op.__exit__(None, None, None)

        ln_es3 = ExitStack()
        xln3 = ln_es3.enter_context(nc.sbuf_tensor("xln3", [128, ET * Q], BF16))
        with nc.named_scope("ph_ln3"):
            ln3_done = emit_ln("l3", xln3, x_ready_dve=outproj_stt,
                               x_ready_scal=(d_scal.n if DEBUG else None))
        with nc.named_scope("ph_ffn2"):
            ffn2_stt, _ = emit_ffn("f2", xln3, ln3_done, outproj_pe, None, None, None)
        ln_es3.close()

        @block.sync
        def _(sy):
            sy.wait_ge(s_dve.h, ffn2_stt)
            for g in range(4):
                d_sync.inc(sy.dma_start(
                    out=out_y[:, :].rearrange("(t p) q -> p t q", p=128)[:, 4 * g:4 * g + 4, :],
                    in_=xT[:, :].rearrange("p (t q) -> p t q", q=Q)[:, 4 * g:4 * g + 4, :]), 16)
            sy.wait_ge(d_sync.h, d_sync.n)
            sy.wait_ge(d_scal.h, d_scal.n)
            sy.wait_ge(d_gp.h, d_gp.n)
            sy.wait_ge(d_pa[0].h, d_pa[0].n)
            sy.wait_ge(d_pa[1].h, d_pa[1].n)
            sy.wait_ge(d_sw.h, d_sw.n)
            sy.wait_ge(d_stq[0].h, d_stq[0].n)
            sy.wait_ge(d_stq[1].h, d_stq[1].n)

    es.close()
    return nc


def _host_prep(inputs):
    x = np.asarray(inputs["L0_attn_x"])[0]
    Wq = np.asarray(inputs["L0_attn_Wq"])
    Wk = np.asarray(inputs["L0_attn_Wk"])
    Wv = np.asarray(inputs["L0_attn_Wv"])
    Wo = np.asarray(inputs["L0_attn_Wo"])
    bq = np.asarray(inputs["L0_attn_bq"])
    bk = np.asarray(inputs["L0_attn_bk"])
    bv = np.asarray(inputs["L0_attn_bv"])
    bo = np.asarray(inputs["L0_attn_bo"])
    lut = np.asarray(inputs["L0_ffn_lut"])
    gidx = np.asarray(inputs["L0_ffn_gate_idx"])
    uidx = np.asarray(inputs["L0_ffn_up_idx"])
    didx = np.asarray(inputs["L0_ffn_down_idx"])

    j = np.arange(64)
    perm = np.empty(E, np.int64)
    for h in range(H):
        perm[h * D + j] = h * D + 2 * j
        perm[h * D + 64 + j] = h * D + 2 * j + 1

    wqT_a = np.ascontiguousarray(Wq[perm].T)
    wkT_a = np.ascontiguousarray(Wk[perm].T)
    wvT_a = np.ascontiguousarray(Wv.T)
    woT_a = np.ascontiguousarray(Wo.T)
    b_t = {
        "q": np.ascontiguousarray(bq[perm].reshape(16, 128).T),
        "k": np.ascontiguousarray(bk[perm].reshape(16, 128).T),
        "v": np.ascontiguousarray(bv.reshape(16, 128).T),
        "o": np.ascontiguousarray(bo.reshape(16, 128).T),
    }
    lutb = np.ascontiguousarray(np.broadcast_to(lut[None, :], (128, 16))).astype(np.float32)
    kl = np.arange(128)[:, None]
    u = np.arange(128)[None, :]
    tri = np.where(u >= kl, 0.0, NEG).astype(np.float32)
    rot = np.zeros((128, 128), np.float32)
    rot[np.arange(128), (np.arange(128) + 64) % 128] = 1.0
    rot = rot.astype(ml_dtypes.bfloat16)

    gidxT = gidx.T
    uidxT = uidx.T
    didxT = didx.T

    th = (10000.0 ** (-(np.arange(128) % 64) / 64.0))[:, None]
    in_maps = []
    for c in range(NCORE):
        pos = (c * Q + np.arange(Q))[None, :].astype(np.float64)
        ang = th * pos
        cosb = np.cos(ang)
        sinb = np.sin(ang)
        sinb[:64] *= -1.0
        in_maps.append({
            "xT_in": np.ascontiguousarray(x[c * Q:(c + 1) * Q, :].T),
            "wqT": wqT_a, "wkT": wkT_a, "wvT": wvT_a, "woT": woT_a,
            "bq_in": b_t["q"], "bk_in": b_t["k"], "bv_in": b_t["v"], "bo_in": b_t["o"],
            "lut_in": lutb,
            "gidx_in": np.ascontiguousarray(gidxT[256 * c:256 * (c + 1), :]),
            "uidx_in": np.ascontiguousarray(uidxT[256 * c:256 * (c + 1), :]),
            "didx_in": np.ascontiguousarray(didxT[688 * c:688 * (c + 1), :]),
            "cos_in": cosb.astype(ml_dtypes.bfloat16),
            "sin_in": sinb.astype(ml_dtypes.bfloat16),
            "tri_in": tri,
            "rot_in": rot,
        })
    return in_maps


def kernel(**inputs):
    in_maps = _host_prep(inputs)
    nc = build_nc()
    res = run_bass_kernel_spmd(nc, in_maps, core_ids=list(range(NCORE)))
    out = np.empty((1, S, E), np.float32)
    for c in range(NCORE):
        out[0, c * Q:(c + 1) * Q, :] = res.results[c]["out_y"].T
    return out


# version 13
# speedup vs baseline: 1.0201x; 1.0201x over previous
import numpy as np
import ml_dtypes
from contextlib import ExitStack

import concourse.bass as bass
import concourse.mybir as mybir
from concourse.bass_utils import run_bass_kernel_spmd
from concourse.expressions import RuntimeValue
from concourse.aot_env import allow_ici_all_to_all

F32 = mybir.dt.float32
BF16 = mybir.dt.bfloat16
F16 = mybir.dt.float16
I32 = mybir.dt.int32
AF = mybir.ActivationFunctionType
ALU = mybir.AluOpType
ds = bass.ds

S, E, H, D = 8192, 2048, 16, 128
NI, NIT, ET = 5504, 43, 16
NCORE = 8
Q = 1024
QC = 512
NQK = S // QC
SCALE = 1.0 / float(np.sqrt(D))
EXPB = -12.0
NEG = -30000.0
EPS = 1e-5


class Ctr:
    __slots__ = ("h", "n")

    def __init__(self, nc, es, name):
        self.h = es.enter_context(nc.semaphore(name))
        self.n = 0

    def inc(self, instr, k=1):
        instr.then_inc(self.h, k)
        self.n += k
        return self.n


DEBUG = False


def build_nc():
    nc = bass.Bass("TRN2", target_bir_lowering=False, dynamic_dma_scratch_size=16384)

    xT_in = nc.dram_tensor("xT_in", [E, Q], F32, kind="ExternalInput")
    wqT = nc.dram_tensor("wqT", [E, E], F32, kind="ExternalInput")
    wkT = nc.dram_tensor("wkT", [E, E], F32, kind="ExternalInput")
    wvT = nc.dram_tensor("wvT", [E, E], F32, kind="ExternalInput")
    woT = nc.dram_tensor("woT", [E, E], F32, kind="ExternalInput")
    bq_in = nc.dram_tensor("bq_in", [128, 16], F32, kind="ExternalInput")
    bk_in = nc.dram_tensor("bk_in", [128, 16], F32, kind="ExternalInput")
    bv_in = nc.dram_tensor("bv_in", [128, 16], F32, kind="ExternalInput")
    bo_in = nc.dram_tensor("bo_in", [128, 16], F32, kind="ExternalInput")
    lut_in = nc.dram_tensor("lut_in", [128, 16], F32, kind="ExternalInput")
    gidx_in = nc.dram_tensor("gidx_in", [256, NI], I32, kind="ExternalInput")
    uidx_in = nc.dram_tensor("uidx_in", [256, NI], I32, kind="ExternalInput")
    didx_in = nc.dram_tensor("didx_in", [688, E], I32, kind="ExternalInput")
    cos_in = nc.dram_tensor("cos_in", [128, Q], BF16, kind="ExternalInput")
    sin_in = nc.dram_tensor("sin_in", [128, Q], BF16, kind="ExternalInput")
    tri_in = nc.dram_tensor("tri_in", [128, 128], F32, kind="ExternalInput")
    rot_in = nc.dram_tensor("rot_in", [128, 128], BF16, kind="ExternalInput")

    out_y = nc.dram_tensor("out_y", [E, Q], F32, kind="ExternalOutput")
    taps = {}
    if DEBUG:
        taps["xln1"] = nc.dram_tensor("tap_xln1", [128, ET * Q], BF16, kind="ExternalOutput")
        taps["x1"] = nc.dram_tensor("tap_x1", [128, ET * Q], F32, kind="ExternalOutput")
        taps["qkv"] = nc.dram_tensor("tap_qkv", [3 * E, Q], BF16, kind="ExternalOutput")
        taps["orr"] = nc.dram_tensor("tap_orr", [E, Q], BF16, kind="ExternalOutput")
        taps["x2"] = nc.dram_tensor("tap_x2", [128, ET * Q], F32, kind="ExternalOutput")
        taps["wg"] = nc.dram_tensor("tap_wg", [E, NI], BF16, kind="ExternalOutput")
        taps["h"] = nc.dram_tensor("tap_h", [128, NIT * QC], BF16, kind="ExternalOutput")
        taps["kt"] = nc.dram_tensor("tap_kt", [128, 2 * S], BF16, kind="ExternalOutput")
        taps["vt"] = nc.dram_tensor("tap_vt", [128, 2 * S], BF16, kind="ExternalOutput")
        taps["gate"] = nc.dram_tensor("tap_gate", [128, QC], F32, kind="ExternalOutput")
        taps["up"] = nc.dram_tensor("tap_up", [128, QC], F32, kind="ExternalOutput")
        taps["wd"] = nc.dram_tensor("tap_wd", [NI, E], BF16, kind="ExternalOutput")
        taps["x1b"] = nc.dram_tensor("tap_x1b", [128, ET * Q], F32, kind="ExternalOutput")

    EH = E // 2
    ag_in = {"g": nc.dram_tensor("ag_g_in", [256, NI], BF16, kind="Internal"),
             "u": nc.dram_tensor("ag_u_in", [256, NI], BF16, kind="Internal"),
             "d0": nc.dram_tensor("ag_d0_in", [688, EH], BF16, kind="Internal"),
             "d1": nc.dram_tensor("ag_d1_in", [688, EH], BF16, kind="Internal")}
    ag_out = {"g": nc.dram_tensor("ag_g_out", [E, NI], BF16, kind="Internal", addr_space="Shared"),
              "u": nc.dram_tensor("ag_u_out", [E, NI], BF16, kind="Internal", addr_space="Shared"),
              "d0": nc.dram_tensor("ag_d0_out", [NI, EH], BF16, kind="Internal", addr_space="Shared"),
              "d1": nc.dram_tensor("ag_d1_out", [NI, EH], BF16, kind="Internal", addr_space="Shared")}
    agkv_in = nc.dram_tensor("agkv_in", [NCORE * 512, Q], BF16, kind="Internal")
    agkv_out = nc.dram_tensor("agkv_out", [NCORE * 512, Q], BF16, kind="Internal")
    agq_in = nc.dram_tensor("agq_in", [NCORE * 256, Q], BF16, kind="Internal")
    agq_out = nc.dram_tensor("agq_out", [NCORE * 256, Q], BF16, kind="Internal")
    ago_in = nc.dram_tensor("ago_in", [E, Q], BF16, kind="Internal")
    ago_out = nc.dram_tensor("ago_out", [E, Q], BF16, kind="Internal")

    es = ExitStack()
    sb = lambda name, shp, dt: es.enter_context(nc.sbuf_tensor(name, shp, dt))

    xT = sb("xT", [128, ET * Q], F32)
    tri_sb = sb("tri_sb", [128, 128], F32)
    lut_sb = sb("lut_sb", [128, 16], F32)
    bias_sb = {p: sb(f"b{p}_sb", [128, 16], F32) for p in "qkvo"}
    ones_cb = sb("ones_cb", [128, 1], BF16)
    ones_ff = sb("ones_ff", [128, 128], F32)
    ones_rf = sb("ones_rf", [1, 128], F32)
    ident = sb("ident", [128, 128], BF16)
    rot_sb = sb("rot_sb", [128, 128], BF16)
    cos_sb = sb("cos_sb", [128, Q], BF16)
    sin_sb = sb("sin_sb", [128, Q], BF16)
    mu_sb = sb("mu_sb", [1, QC], F32)
    var_sb = sb("var_sb", [1, QC], F32)
    std_sb = sb("std_sb", [1, QC], F32)
    rs_sb = sb("rs_sb", [1, QC], F32)
    bvv_sb = sb("bvv_sb", [1, QC], F32)
    A_sb = sb("A_sb", [128, QC], F32)
    B_sb = sb("B_sb", [128, QC], F32)
    xnt_sb = sb("xnt_sb", [128, QC], F32)
    eps_sb = sb("eps_sb", [128, 1], F32)
    expb_sb = sb("expb_sb", [128, 1], F32)
    R_sb = sb("R_sb", [128, QC], F32)
    o_sb = sb("o_sb", [128, QC], BF16)
    la_sb = xnt_sb
    lb_sb = A_sb

    s_pe = Ctr(nc, es, "s_pe")
    s_dve = Ctr(nc, es, "s_dve")
    s_act = Ctr(nc, es, "s_act")
    d_sync = Ctr(nc, es, "d_sync")
    d_scal = Ctr(nc, es, "d_scal")
    d_gp = Ctr(nc, es, "d_gp")
    s_cc = Ctr(nc, es, "s_cc")
    d_pa = [Ctr(nc, es, "d_pa0"), Ctr(nc, es, "d_pa1")]
    d_sw = Ctr(nc, es, "d_sw")
    d_stq = [Ctr(nc, es, "d_stq0"), Ctr(nc, es, "d_stq1")]
    s_gp = Ctr(nc, es, "s_gp")
    d_mg = [Ctr(nc, es, "d_mg0"), Ctr(nc, es, "d_mg1")]
    d_gst = Ctr(nc, es, "d_gst")

    with nc.Block() as block:
        @block.sync
        def _(sy):
            for dst, src in ((tri_sb, tri_in), (lut_sb, lut_in), (cos_sb, cos_in),
                             (sin_sb, sin_in), (bias_sb["q"], bq_in), (bias_sb["k"], bk_in),
                             (bias_sb["v"], bv_in), (bias_sb["o"], bo_in),
                             (rot_sb, rot_in)):
                d_sync.inc(sy.dma_start(out=dst[:, :], in_=src[:, :]), 16)
            for g in range(4):
                d_sync.inc(sy.dma_start(
                    out=xT[:, :].rearrange("p (t q) -> p t q", q=Q)[:, 4 * g:4 * g + 4, :],
                    in_=xT_in[:, :].rearrange("(t p) q -> p t q", p=128)[:, 4 * g:4 * g + 4, :]), 16)
        x_ready = d_sync.n

        @block.gpsimd
        def _(gp):
            gp.memset(ones_cb[:, :], 1.0)
            gp.memset(ones_ff[:, :], 1.0)
            gp.memset(ones_rf[:, :], 1.0)
            s_dve.inc(gp.memset(eps_sb[:, :], EPS), 1)
            s_dve.inc(gp.memset(expb_sb[:, :], EXPB), 1)
            s_dve.inc(gp.memset(ident[:, :], 1.0), 1)
            gp.wait_ge(s_dve.h, s_dve.n)
            s_dve.inc(gp.affine_select(ident[:, :], ident[:, :], [[1, 128]],
                                       ALU.is_equal, 0.0, base=0, channel_multiplier=-1), 1)

        def emit_ln(tag, xln, x_ready_dve=None, x_ready_dma=None, x_ready_scal=None,
                    qcs=(0, 1)):
            pse = ExitStack()
            psM = pse.enter_context(nc.psum_tensor(f"psM{tag}", [1, QC], F32))
            psV = pse.enter_context(nc.psum_tensor(f"psV{tag}", [1, QC], F32))
            psA = pse.enter_context(nc.psum_tensor(f"psA{tag}", [128, QC], F32))
            psB = pse.enter_context(nc.psum_tensor(f"psB{tag}", [128, QC], F32))
            done = {}
            prev_var_dve = None
            for qc in qcs:
                def xsl(et, qc=qc):
                    return xT[:, et * Q + qc * QC: et * Q + qc * QC + QC]

                def lsl(et, qc=qc, xln=xln):
                    return xln[:, et * Q + qc * QC: et * Q + qc * QC + QC]

                pv = prev_var_dve

                @block.vector
                def _(v):
                    if x_ready_dve is not None:
                        v.wait_ge(s_dve.h, x_ready_dve)
                    if x_ready_dma is not None:
                        v.wait_ge(d_sync.h, x_ready_dma)
                    if x_ready_scal is not None:
                        v.wait_ge(d_scal.h, x_ready_scal)
                    last = None
                    for et in range(ET):
                        last = v.tensor_copy(lsl(et), xsl(et))
                    s_dve.inc(last, 1)
                cast_all = s_dve.n

                @block.tensor
                def _(t):
                    t.wait_ge(s_dve.h, cast_all)
                    if pv is not None:
                        t.wait_ge(s_dve.h, pv)
                    last = None
                    for et in range(ET):
                        last = t.matmul(psM[:, :], ones_cb[:, :], lsl(et),
                                        start=(et == 0), stop=(et == ET - 1))
                    s_pe.inc(last, 1)
                mean_pe = s_pe.n

                @block.vector
                def _(v):
                    v.wait_ge(s_pe.h, mean_pe)
                    last = None
                    for et in range(ET):
                        last = v.tensor_tensor(lsl(et), lsl(et), lsl(et), ALU.mult)
                    s_dve.inc(last, 1)
                sq_all = s_dve.n

                @block.tensor
                def _(t):
                    t.wait_ge(s_dve.h, sq_all)
                    last = None
                    for et in range(ET):
                        last = t.matmul(psV[:, :], ones_cb[:, :], lsl(et),
                                        start=(et == 0), stop=(et == ET - 1))
                    s_pe.inc(last, 1)
                stats_pe = s_pe.n

                @block.vector
                def _(v):
                    v.wait_ge(s_pe.h, stats_pe)
                    v.tensor_scalar(mu_sb[:, :], psM[:, :], 1.0 / E, None, ALU.mult)
                    v.tensor_scalar(var_sb[:, :], psV[:, :], 1.0 / E, None, ALU.mult)
                    v.tensor_tensor(bvv_sb[:, :], mu_sb[:, :], mu_sb[:, :], ALU.mult)
                    s_dve.inc(v.tensor_tensor(var_sb[:, :], var_sb[:, :], bvv_sb[:, :],
                                              ALU.subtract), 1)
                var_dve = s_dve.n
                prev_var_dve = var_dve

                @block.scalar
                def _(sc):
                    sc.wait_ge(s_dve.h, var_dve)
                    s_act.inc(sc.activation(std_sb[:, :], var_sb[:, :], AF.Sqrt,
                                            bias=eps_sb[0:1, :], scale=1.0), 1)
                std_act = s_act.n

                @block.vector
                def _(v):
                    v.wait_ge(s_act.h, std_act)
                    v.reciprocal(rs_sb[:, :], std_sb[:, :])
                    v.tensor_scalar(bvv_sb[:, :], mu_sb[:, :], -1.0, None, ALU.mult)
                    s_dve.inc(v.tensor_tensor(bvv_sb[:, :], bvv_sb[:, :], rs_sb[:, :],
                                              ALU.mult), 1)
                ab_dve = s_dve.n

                @block.tensor
                def _(t):
                    t.wait_ge(s_dve.h, ab_dve)
                    t.matmul(psA[:, :], ones_rf[:, :], rs_sb[:, :], start=True, stop=True)
                    s_pe.inc(t.matmul(psB[:, :], ones_rf[:, :], bvv_sb[:, :],
                                      start=True, stop=True), 1)
                bc_pe = s_pe.n

                @block.vector
                def _(v):
                    v.wait_ge(s_pe.h, bc_pe)
                    v.tensor_copy(A_sb[:, :], psA[:, :])
                    v.tensor_copy(B_sb[:, :], psB[:, :])
                    last = None
                    for et in range(ET):
                        v.tensor_tensor(xnt_sb[:, :], xsl(et), A_sb[:, :], ALU.mult)
                        last = v.tensor_tensor(lsl(et), xnt_sb[:, :], B_sb[:, :], ALU.add)
                    done[qc] = s_dve.inc(last, 1)
            pse.close()
            return done

        def emit_ffn(tag, xln, xln_done, entry_pe_wait, entry_cc_g, entry_cc_u,
                     entry_cc_d):
            last_stt = None
            prev_qc_B_pe = None
            for qc in range(2):
                qoff = qc * QC
                fse = ExitStack()
                h_sb = fse.enter_context(nc.sbuf_tensor(f"h{tag}{qc}", [128, NIT * QC], F16))
                ase = ExitStack()
                wa_buf = [ase.enter_context(nc.sbuf_tensor(f"wa{tag}{qc}{i}", [128, ET * 128], BF16)) for i in range(2)]
                pse = ExitStack()
                psP = [pse.enter_context(nc.psum_tensor(f"psP{tag}{qc}{i}", [128, QC], F32)) for i in range(2)]
                psH = [pse.enter_context(nc.psum_tensor(f"psH{tag}{qc}{i}", [128, QC], F32)) for i in range(2)]

                passes = [("g", entry_cc_g), ("u", entry_cc_u)]
                ldA_sched, mmA_sched = {}, {}
                _pa = [d_pa[0].n, d_pa[1].n]
                k = 0
                for pname, _cc in passes:
                    for si in range(NIT):
                        _pa[k % 2] += 16
                        ldA_sched[(pname, si)] = _pa[k % 2]
                        k += 1
                npe = s_pe.n
                for pname, _cc in passes:
                    for si in range(NIT):
                        npe += 1
                        mmA_sched[(pname, si)] = npe
                silu_sched = {si: s_act.n + (si + 1) for si in range(NIT)}
                h_sched = {si: s_dve.n + (si + 1) for si in range(NIT)}
                pqB = prev_qc_B_pe

                @block.sync
                def _(sy):
                    prev_mm = [None, None]
                    k = 0
                    for pname, ccm in passes:
                        for si in range(NIT):
                            p = k % 2
                            if pname == "g" and si == 0:
                                if ccm is not None:
                                    sy.wait_ge(s_cc.h, ccm)
                                if qc == 0 and entry_pe_wait is not None:
                                    sy.wait_ge(s_pe.h, entry_pe_wait)
                                if qc == 1 and pqB is not None:
                                    sy.wait_ge(s_pe.h, pqB)
                            if pname == "u" and si == 0 and ccm is not None:
                                sy.wait_ge(s_cc.h, ccm)
                            if prev_mm[p] is not None:
                                sy.wait_ge(s_pe.h, prev_mm[p])
                            prev_mm[p] = mmA_sched[(pname, si)]
                            c0 = si * 128
                            mk = d_pa[p].inc(sy.dma_start(
                                out=wa_buf[p][:, :].rearrange("p (e c) -> p e c", c=128),
                                in_=ag_out[pname][:, c0:c0 + 128].rearrange("(e p) c -> p e c", p=128)), 16)
                            assert mk == ldA_sched[(pname, si)]
                            k += 1

                @block.tensor
                def _(t):
                    k = 0
                    for pname, _cc in passes:
                        for si in range(NIT):
                            p = k % 2
                            if pname == "g" and si == 0:
                                t.wait_ge(s_dve.h, xln_done[qc])
                            if pname == "u" and si == 0 and entry_cc_g is None:
                                t.wait_ge(s_dve.h, xln_done[1])
                            t.wait_ge(d_pa[p].h, ldA_sched[(pname, si)])
                            if pname == "g" and si >= 2:
                                t.wait_ge(s_act.h, silu_sched[si - 2])
                            if pname == "u" and si >= 2:
                                t.wait_ge(s_dve.h, h_sched[si - 2])
                            ps = psP if pname == "g" else psH
                            last = None
                            for et in range(ET):
                                last = t.matmul(ps[si % 2][:, :],
                                                wa_buf[p][:, et * 128:(et + 1) * 128],
                                                xln[:, et * Q + qoff: et * Q + qoff + QC],
                                                start=(et == 0), stop=(et == ET - 1))
                            assert s_pe.inc(last, 1) == mmA_sched[(pname, si)]
                            k += 1

                @block.scalar
                def _(sc):
                    for si in range(NIT):
                        sc.wait_ge(s_pe.h, mmA_sched[("g", si)])
                        mk = sc.activation(h_sb[:, si * QC:(si + 1) * QC], psP[si % 2][:, :],
                                           AF.Silu, bias=0.0, scale=1.0)
                        assert s_act.inc(mk, 1) == silu_sched[si]

                @block.vector
                def _(v):
                    for si in range(NIT):
                        v.wait_ge(s_pe.h, mmA_sched[("u", si)])
                        v.wait_ge(s_act.h, silu_sched[si])
                        mk = v.tensor_tensor(h_sb[:, si * QC:(si + 1) * QC],
                                             h_sb[:, si * QC:(si + 1) * QC],
                                             psH[si % 2][:, :], ALU.mult)
                        assert s_dve.inc(mk, 1) == h_sched[si]

                phaseA_pe = s_pe.n
                h_all = s_dve.n
                ase.close()
                pse.close()

                bse = ExitStack()
                wd_buf = [bse.enter_context(nc.sbuf_tensor(f"wd{tag}{qc}{i}", [128, 4 * 1024], BF16)) for i in range(2)]
                ps2 = ExitStack()
                psY = [ps2.enter_context(nc.psum_tensor(f"psY{tag}{qc}{i}", [128, QC], F32)) for i in range(8)]

                groups = []
                it0 = 0
                while it0 < NIT:
                    groups.append((it0, min(4, NIT - it0)))
                    it0 += 4
                NG = len(groups)
                bsteps = [(hf, gi) for hf in range(2) for gi in range(NG)]
                ldB_sched = {}
                _pb = [d_pa[0].n, d_pa[1].n]
                for (hf, gi) in bsteps:
                    p = gi % 2
                    _pb[p] += 16
                    ldB_sched[(hf, gi)] = _pb[p]
                mmB_sched = {s: s_pe.n + (i + 1) for i, s in enumerate(bsteps)}
                stt_sched = {hf: s_dve.n + 8 * (hf + 1) for hf in range(2)}

                @block.sync
                def _(sy):
                    last_par = {0: None, 1: None}
                    for i, (hf, gi) in enumerate(bsteps):
                        it0, ng = groups[gi]
                        if gi == 0:
                            if entry_cc_d is not None:
                                sy.wait_ge(s_cc.h, entry_cc_d[hf])
                            if i == 0:
                                sy.wait_ge(s_dve.h, h_all)
                        p = gi % 2
                        if last_par[p] is not None:
                            sy.wait_ge(s_pe.h, mmB_sched[bsteps[last_par[p]]])
                        last_par[p] = i
                        mk = d_pa[p].inc(sy.dma_start(
                            out=wd_buf[gi % 2][:, 0:ng * 1024].rearrange("p (t c) -> p t c", c=1024),
                            in_=ag_out[f"d{hf}"][it0 * 128:(it0 + ng) * 128,
                                                 :].rearrange("(t p) c -> p t c", p=128)), 16)
                        assert mk == ldB_sched[(hf, gi)]

                @block.tensor
                def _(t):
                    for (hf, gi) in bsteps:
                        it0, ng = groups[gi]
                        if gi == 0:
                            t.wait_ge(s_dve.h, h_all)
                            if hf == 1:
                                t.wait_ge(s_dve.h, stt_sched[0])
                        t.wait_ge(d_pa[gi % 2].h, ldB_sched[(hf, gi)])
                        last = None
                        for il in range(ng):
                            for eo in range(8):
                                last = t.matmul(psY[eo][:, :],
                                                wd_buf[gi % 2][:, il * 1024 + eo * 128: il * 1024 + (eo + 1) * 128],
                                                h_sb[:, (it0 + il) * QC:(it0 + il + 1) * QC],
                                                start=(it0 + il == 0), stop=(it0 + il == NIT - 1))
                        assert s_pe.inc(last, 1) == mmB_sched[(hf, gi)]

                @block.vector
                def _(v):
                    for hf in range(2):
                        v.wait_ge(s_pe.h, mmB_sched[(hf, NG - 1)])
                        mk = None
                        for eo in range(8):
                            eog = hf * 8 + eo
                            xsl = xT[:, eog * Q + qoff: eog * Q + qoff + QC]
                            mk = v.scalar_tensor_tensor(xsl, psY[eo][:, :], 0.5, xsl,
                                                        ALU.mult, ALU.add)
                        assert s_dve.inc(mk, 8) == stt_sched[hf]

                last_stt = s_dve.n
                prev_qc_B_pe = s_pe.n
                ps2.close()
                bse.close()
                fse.close()
            return last_stt, s_pe.n

        _sc_mat = nc.named_scope("ph_mat"); _sc_mat.__enter__()
        ln_es1 = ExitStack()
        xln1 = ln_es1.enter_context(nc.sbuf_tensor("xln1", [128, ET * Q], BF16))
        mat_es = ExitStack()
        NIQ = NI // 4
        midx = [mat_es.enter_context(nc.sbuf_tensor(f"midx{i}", [128, NIQ], I32))
                for i in range(2)]
        midx16 = mat_es.enter_context(nc.sbuf_tensor("midx16", [128, NIQ], F16))
        mt_sb = mat_es.enter_context(nc.sbuf_tensor("mt_sb", [128, NIQ], BF16))
        macc = mat_es.enter_context(nc.sbuf_tensor("macc", [128, NIQ], BF16))

        idx_srcs = {"g": gidx_in, "u": uidx_in, "d0": didx_in, "d1": didx_in}

        _prev_mat_done = [None, 0]

        def emit_mat_part(tiles, ag_key):
            NT = len(tiles)
            mat_ld = {}
            _ml = [d_mg[0].n, d_mg[1].n]
            mat_par = {}
            pi0 = _prev_mat_done[1]
            for i in range(NT):
                p = (pi0 + i) % 2
                _ml[p] += 16
                mat_ld[i] = _ml[p]
                mat_par[i] = p
            mat_done = {i: s_dve.n + (i + 1) for i in range(NT)}
            mat_st = {i: d_gst.n + 16 * (i + 1) for i in range(NT)}
            pd = _prev_mat_done[0]

            @block.gpsimd
            def _(gp):
                for i, (mm_, r0, pc, c0, dc0, w) in enumerate(tiles):
                    prev = mat_done[i - 2] if i > 1 else (pd if i > 0 else None)
                    if i == 0 and pd is not None:
                        pass
                    if prev is not None:
                        gp.wait_ge(s_dve.h, prev)
                    mk = d_mg[mat_par[i]].inc(gp.dma_start(
                        out=midx[mat_par[i]][0:pc, 0:w],
                        in_=idx_srcs[mm_][r0:r0 + pc, c0:c0 + w]), 16)
                    assert mk == mat_ld[i]
                    if i > 0:
                        gp.wait_ge(s_dve.h, mat_done[i - 1])
                        pm = tiles[i - 1]
                        mk = d_gst.inc(gp.dma_start(
                            out=ag_in[pm[0]][pm[1]:pm[1] + pm[2], pm[4]:pm[4] + pm[5]],
                            in_=macc[0:pm[2], 0:pm[5]]), 16)
                        assert mk == mat_st[i - 1]
                lm = tiles[NT - 1]
                gp.wait_ge(s_dve.h, mat_done[NT - 1])
                mk = d_gst.inc(gp.dma_start(
                    out=ag_in[lm[0]][lm[1]:lm[1] + lm[2], lm[4]:lm[4] + lm[5]],
                    in_=macc[0:lm[2], 0:lm[5]]), 16)
                assert mk == mat_st[NT - 1]
                gp.wait_ge(d_gst.h, mat_st[NT - 1])
                cc_w[ag_key] = s_cc.inc(gp.collective_compute(
                    "AllGather", ALU.bypass, replica_groups=[list(range(NCORE))],
                    ins=[ag_in[ag_key].ap().opt()], outs=[ag_out[ag_key].ap().opt()]), 1)

            @block.vector
            def _(v):
                for i, (mm_, r0, pc, c0, dc0, w) in enumerate(tiles):
                    v.wait_ge(d_mg[mat_par[i]].h, mat_ld[i])
                    if i > 0:
                        v.wait_ge(d_gst.h, mat_st[i - 1])
                    v.tensor_copy(midx16[0:pc, 0:w], midx[mat_par[i]][0:pc, 0:w])
                    v.tensor_scalar(macc[0:pc, 0:w], midx16[0:pc, 0:w], 0.0,
                                    lut_sb[0:pc, 0:1], ALU.is_equal, ALU.mult)
                    last = None
                    for k in range(1, 16):
                        v.tensor_scalar(mt_sb[0:pc, 0:w], midx16[0:pc, 0:w], float(k),
                                        lut_sb[0:pc, k:k + 1], ALU.is_equal, ALU.mult)
                        last = v.tensor_tensor(macc[0:pc, 0:w], macc[0:pc, 0:w],
                                               mt_sb[0:pc, 0:w], ALU.add)
                    assert s_dve.inc(last, 1) == mat_done[i]

            _prev_mat_done[0] = mat_done[NT - 1]
            _prev_mat_done[1] = (pi0 + NT) % 2
            return mat_st[NT - 1]

        cc_w = {}

        def _gu_tiles(m):
            return [(m, r0, 128, c0, c0, NIQ) for r0 in (0, 128)
                    for c0 in (0, NIQ, 2 * NIQ, 3 * NIQ)]

        def _d_tiles(half):
            return [(f"d{half}", r0, min(128, 688 - r0), half * EH + c0, c0, EH // 2)
                    for r0 in range(0, 688, 128) for c0 in (0, EH // 2)]

        g_st = emit_mat_part(_gu_tiles("g"), "g")
        with nc.named_scope("ph_ln1"):
            ln1_done = emit_ln("l1a", xln1, x_ready_dma=x_ready, qcs=(0,))
            res_ps = ExitStack()
            for _i in range(4):
                res_ps.enter_context(nc.psum_tensor(f"resv{_i}", [128, QC], F32))
            ln1_done.update(emit_ln("l1b", xln1, x_ready_dma=x_ready, qcs=(1,)))
            res_ps.close()
        u_st = emit_mat_part(_gu_tiles("u"), "u")
        d0_st = emit_mat_part(_d_tiles(0), "d0")
        d1_st = emit_mat_part(_d_tiles(1), "d1")

        _sc_mat.__exit__(None, None, None)

        if DEBUG:
            @block.scalar
            def _(sc):
                sc.wait_ge(s_dve.h, ln1_done[1])
                d_scal.inc(sc.dma_start(out=taps["xln1"][:, :], in_=xln1[:, :]), 16)

            @block.gpsimd
            def _(gp):
                gp.wait_ge(s_cc.h, cc_w["d1"])
                d_gp.inc(gp.dma_start(out=taps["wg"][:, :], in_=ag_out["g"][:, :]), 16)
                d_gp.inc(gp.dma_start(out=taps["wd"][:, 0:EH], in_=ag_out["d0"][:, :]), 16)
                d_gp.inc(gp.dma_start(out=taps["wd"][:, EH:E], in_=ag_out["d1"][:, :]), 16)
        with nc.named_scope("ph_ffn1"):
            ffn1_stt, ffn1_pe = emit_ffn("f1", xln1, ln1_done, None,
                                         cc_w["g"], cc_w["u"],
                                         (cc_w["d0"], cc_w["d1"]))
        mat_es.close()
        ln_es1.close()
        if DEBUG:
            @block.scalar
            def _(sc):
                sc.wait_ge(s_dve.h, ffn1_stt)
                d_scal.inc(sc.dma_start(out=taps["x1"][:, :], in_=xT[:, :]), 16)

        ln_es2 = ExitStack()
        xln2 = ln_es2.enter_context(nc.sbuf_tensor("xln2", [128, ET * Q], BF16))
        with nc.named_scope("ph_ln2"):
            ln2_done = emit_ln("l2", xln2, x_ready_dve=ffn1_stt)

        _sc_qkv = nc.named_scope("ph_qkv"); _sc_qkv.__enter__()
        qkv_es = ExitStack()
        wq_buf = [qkv_es.enter_context(nc.sbuf_tensor(f"wqb{i}", [128, ET * 128], BF16)) for i in range(2)]
        qt_tmp = [qkv_es.enter_context(nc.sbuf_tensor(f"qt{i}", [128, Q], BF16)) for i in range(2)]
        rt_tmp = qkv_es.enter_context(nc.sbuf_tensor("rt", [128, Q], BF16))
        qkv_ps = ExitStack()
        psQ = [qkv_ps.enter_context(nc.psum_tensor(f"psQ{i}", [128, QC], F32)) for i in range(2)]
        psSW = [qkv_ps.enter_context(nc.psum_tensor(f"psSW{i}", [128, Q], F32)) for i in range(2)]
        qkv_entry_pe = ffn1_pe

        projs = [("k", wkT, 0), ("v", wvT, 256), ("q", wqT, 0)]
        NPF = 3 * ET
        qmm_sched, swmm_sched = {}, {}
        npe = s_pe.n
        for i in range(NPF):
            pname = projs[i // ET][0]
            npe += 1
            qmm_sched[i] = npe
            if pname in ("q", "k"):
                npe += 1
                swmm_sched[i] = npe
        qev_sched, qrp_sched, qld_sched, qst_sched = {}, {}, {}, {}
        nd = s_dve.n
        _pq = [d_pa[0].n, d_pa[1].n]
        _sq = [d_stq[0].n, d_stq[1].n]
        for i in range(NPF):
            pname = projs[i // ET][0]
            _pq[i % 2] += 16
            qld_sched[i] = _pq[i % 2]
            nd += 1
            qev_sched[i] = nd
            if pname in ("q", "k"):
                nd += 1
                qrp_sched[i] = nd
            _sq[i % 2] += 16
            qst_sched[i] = _sq[i % 2]

        @block.gpsimd
        def _(gp):
            for i in range(NPF):
                pi, ft = divmod(i, ET)
                pname, wt, _off = projs[pi]
                if i == 0:
                    gp.wait_ge(s_pe.h, qkv_entry_pe)
                if i >= 2:
                    gp.wait_ge(s_pe.h, qmm_sched[i - 2])
                src_ap = wt[:, ft * 128:(ft + 1) * 128].rearrange("(t p) c -> p t c", p=128)
                mk = d_pa[i % 2].inc(gp.dma_start(
                    out=wq_buf[i % 2][:, :].rearrange("p (e c) -> p e c", c=128),
                    in_=src_ap), 16)
                assert mk == qld_sched[i]

        @block.tensor
        def _(t):
            for i in range(NPF):
                pi, ft = divmod(i, ET)
                pname = projs[pi][0]
                if i == 0:
                    t.wait_ge(s_dve.h, ln2_done[1])
                t.wait_ge(d_pa[i % 2].h, qld_sched[i])
                if i >= 1:
                    t.wait_ge(s_dve.h, qev_sched[i - 1])
                if i >= 2:
                    t.wait_ge(d_stq[i % 2].h, qst_sched[i - 2])
                last = None
                for et in range(ET):
                    for qc in range(2):
                        last = t.matmul(psQ[qc][:, :],
                                        wq_buf[i % 2][:, et * 128:(et + 1) * 128],
                                        xln2[:, et * Q + qc * QC: et * Q + qc * QC + QC],
                                        start=(et == 0), stop=(et == ET - 1))
                assert s_pe.inc(last, 1) == qmm_sched[i]
                if pname in ("q", "k"):
                    t.wait_ge(s_dve.h, qev_sched[i])
                    pr = [j for j in range(i - 1, -1, -1)
                          if j % 2 == i % 2 and j in qrp_sched]
                    if pr:
                        t.wait_ge(s_dve.h, qrp_sched[pr[0]])
                    t.matmul(psSW[i % 2][:, 0:QC], rot_sb[:, :], qt_tmp[i % 2][:, 0:QC],
                             start=True, stop=True)
                    mk = t.matmul(psSW[i % 2][:, QC:Q], rot_sb[:, :], qt_tmp[i % 2][:, QC:Q],
                                  start=True, stop=True)
                    assert s_pe.inc(mk, 1) == swmm_sched[i]

        @block.vector
        def _(v):
            for i in range(NPF):
                pi, ft = divmod(i, ET)
                pname = projs[pi][0]
                v.wait_ge(s_pe.h, qmm_sched[i])
                v.tensor_scalar(qt_tmp[i % 2][:, 0:QC], psQ[0][:, :],
                                bias_sb[pname][:, ft:ft + 1], None, ALU.add)
                mk = v.tensor_scalar(qt_tmp[i % 2][:, QC:Q], psQ[1][:, :],
                                     bias_sb[pname][:, ft:ft + 1], None, ALU.add)
                assert s_dve.inc(mk, 1) == qev_sched[i]
                if pname in ("q", "k"):
                    v.wait_ge(s_pe.h, swmm_sched[i])
                    v.tensor_tensor(rt_tmp[:, :], psSW[i % 2][:, :], sin_sb[:, :], ALU.mult)
                    v.tensor_tensor(qt_tmp[i % 2][:, :], qt_tmp[i % 2][:, :], cos_sb[:, :], ALU.mult)
                    mk = v.tensor_tensor(qt_tmp[i % 2][:, :], qt_tmp[i % 2][:, :],
                                         rt_tmp[:, :], ALU.add)
                    assert s_dve.inc(mk, 1) == qrp_sched[i]

        @block.scalar
        def _(sc):
            for i in range(NPF):
                pi, ft = divmod(i, ET)
                pname, _w, off = projs[pi]
                sc.wait_ge(s_dve.h, qrp_sched.get(i, qev_sched[i]))
                if pname == "q":
                    r0 = (ft // 2) * 256 + (ft % 2) * 128
                    dst = agq_in
                else:
                    r0 = (ft // 2) * 512 + off + (ft % 2) * 128
                    dst = agkv_in
                mk = d_stq[i % 2].inc(sc.dma_start(
                    out=dst[r0:r0 + 128, :],
                    in_=qt_tmp[i % 2][:, :]), 16)
                assert mk == qst_sched[i]

        kv_last = {p: max(qst_sched[i] for i in range(2 * ET) if i % 2 == p)
                   for p in range(2)}

        @block.gpsimd
        def _(gp):
            gp.wait_ge(d_stq[0].h, kv_last[0])
            gp.wait_ge(d_stq[1].h, kv_last[1])
            with allow_ici_all_to_all():
                s_cc.inc(gp.collective_compute(
                    "AllToAll", ALU.bypass, replica_groups=[list(range(NCORE))],
                    ins=[agkv_in.ap().opt()], outs=[agkv_out.ap().opt()]), 1)
            gp.wait_ge(d_stq[0].h, d_stq[0].n)
            gp.wait_ge(d_stq[1].h, d_stq[1].n)
            with allow_ici_all_to_all():
                s_cc.inc(gp.collective_compute(
                    "AllToAll", ALU.bypass, replica_groups=[list(range(NCORE))],
                    ins=[agq_in.ap().opt()], outs=[agq_out.ap().opt()]), 1)

        if DEBUG:
            @block.gpsimd
            def _(gp):
                gp.wait_ge(s_cc.h, s_cc.n)
                d_gp.inc(gp.dma_start(out=taps["qkv"][:, :], in_=agq_in[:, :]), 16)
        cc_q = s_cc.n
        cc_kv = cc_q - 1
        qkv_ps.close()
        qkv_es.close()
        ln_es2.close()
        _sc_qkv.__exit__(None, None, None)

        _sc_at = nc.named_scope("ph_attn"); _sc_at.__enter__()
        at_es = ExitStack()
        KT_sb = at_es.enter_context(nc.sbuf_tensor("KT_sb", [128, 2 * S], BF16))
        V_sb = at_es.enter_context(nc.sbuf_tensor("V_sb", [128, 2 * S], BF16))
        VT_tmp = [at_es.enter_context(nc.sbuf_tensor(f"VT{i}", [128, Q], BF16)) for i in range(2)]
        QT_sb = [at_es.enter_context(nc.sbuf_tensor(f"QT{i}", [128, QC], BF16)) for i in range(2)]
        PT_sb = [at_es.enter_context(nc.sbuf_tensor(f"PT{i}", [128, QC], BF16)) for i in range(4)]
        at_ps = ExitStack()
        psS = [at_ps.enter_context(nc.psum_tensor(f"psS{i}", [128, QC], F32)) for i in range(4)]
        psO = [at_ps.enter_context(nc.psum_tensor(f"psO{i}", [128, QC], F32)) for i in range(2)]
        psR = at_ps.enter_context(nc.psum_tensor("psR", [128, QC], F32))
        psT = at_ps.enter_context(nc.psum_tensor("psT", [128, Q], BF16))

        kvt = [(hl, c) for hl in range(2) for c in range(NCORE)]
        kv_ld_sched = {t: d_gp.n + 16 * (i + 1) for i, t in enumerate(kvt)}
        vt_ld_sched = {}
        _pv = [d_pa[0].n, d_pa[1].n]
        for i, t in enumerate(kvt):
            _pv[i % 2] += 16
            vt_ld_sched[t] = _pv[i % 2]
        vt_par = {t: (i % 2) for i, t in enumerate(kvt)}
        vtr_sched = {t: s_pe.n + (i + 1) for i, t in enumerate(kvt)}
        vev_sched = {t: s_dve.n + (i + 1) for i, t in enumerate(kvt)}

        @block.gpsimd
        def _(gp):
            gp.wait_ge(s_cc.h, cc_kv)
            for (hl, c) in kvt:
                r0 = c * 512 + hl * 128
                mk = d_gp.inc(gp.dma_start(
                    out=KT_sb[:, hl * S + c * Q: hl * S + (c + 1) * Q],
                    in_=agkv_out[r0:r0 + 128, :]), 16)
                assert mk == kv_ld_sched[(hl, c)]
            for i, (hl, c) in enumerate(kvt):
                if i >= 2:
                    gp.wait_ge(s_pe.h, vtr_sched[kvt[i - 2]])
                r0 = c * 512 + 256 + hl * 128
                mk = d_pa[i % 2].inc(gp.dma_start(
                    out=VT_tmp[i % 2][:, :],
                    in_=agkv_out[r0:r0 + 128, :]), 16)
                assert mk == vt_ld_sched[(hl, c)]

        @block.tensor
        def _(t):
            for i, (hl, c) in enumerate(kvt):
                t.wait_ge(d_pa[vt_par[(hl, c)]].h, vt_ld_sched[(hl, c)])
                if i >= 1:
                    t.wait_ge(s_dve.h, vev_sched[kvt[i - 1]])
                last = None
                for b in range(8):
                    last = t.transpose(psT[:, b * 128:(b + 1) * 128],
                                       VT_tmp[i % 2][:, b * 128:(b + 1) * 128],
                                       ident[:, :])
                assert s_pe.inc(last, 1) == vtr_sched[(hl, c)]

        @block.vector
        def _(v):
            for (hl, c) in kvt:
                v.wait_ge(s_pe.h, vtr_sched[(hl, c)])
                mk = v.tensor_copy(V_sb[:, hl * S + c * Q: hl * S + (c + 1) * Q], psT[:, :])
                assert s_dve.inc(mk, 1) == vev_sched[(hl, c)]

        chunks = [(hl, qk) for hl in range(2) for qk in range(NQK)]
        _on_gp = lambda kb: (kb % 5) in (1, 3)
        DEP = 4
        gb, gidx = [], {}
        for ci, (hl, qk) in enumerate(chunks):
            for kb in range(4 * qk + 4):
                gidx[(hl, qk, kb)] = len(gb)
                gb.append((ci, hl, qk, kb))
        NG = len(gb)
        qt_ld_sched, qt_par = {}, {}
        mmS_sched, exp_sched, pvl_sched, mask_sched = {}, {}, {}, {}
        l_sched, l_last, lsum_sched = {}, {}, {}
        rec_sched, oev_sched, ost_sched = {}, {}, {}
        npe, nact, ndv, ngp, nsc = s_pe.n, s_act.n, s_dve.n, s_gp.n, d_scal.n
        _pt = [d_pa[0].n, d_pa[1].n]
        for ci, (hl, qk) in enumerate(chunks):
            _pt[ci % 2] += 16
            qt_ld_sched[(hl, qk)] = _pt[ci % 2]
            qt_par[(hl, qk)] = ci % 2
        for gi in range(NG + DEP - 1):
            if gi < NG:
                ci, hl, qk, kb = gb[gi]
                npe += 1
                mmS_sched[(hl, qk, kb)] = npe
            pj = gi - (DEP - 1)
            if pj >= 0:
                pci, phl, pqk, pkb = gb[pj]
                npe += 1
                pvl_sched[(phl, pqk, pkb)] = npe
                if pkb == 4 * pqk + 3:
                    npe += 2
                    lsum_sched[(phl, pqk)] = npe
        for ci, (hl, qk) in enumerate(chunks):
            nb = 4 * qk + 4
            l_last[(hl, qk)] = {e: max(k for k in range(nb)
                                       if (_on_gp(k) == (e == "g"))) for e in "dg"}
            for kb in range(nb):
                nact += 1
                exp_sched[(hl, qk, kb)] = nact
                if kb >= 4 * qk:
                    ndv += 1
                    mask_sched[(hl, qk, kb)] = ndv
                if _on_gp(kb):
                    ngp += 1
                    l_sched[(hl, qk, kb)] = ("g", ngp)
                else:
                    ndv += 1
                    l_sched[(hl, qk, kb)] = ("d", ndv)
                if kb == 2 and ci >= 1:
                    ndv += 1
                    rec_sched[chunks[ci - 1]] = ndv
                    ndv += 1
                    oev_sched[chunks[ci - 1]] = ndv
            if ci >= 1:
                nsc += 16
                ost_sched[chunks[ci - 1]] = nsc
        ndv += 1
        rec_sched[chunks[-1]] = ndv
        ndv += 1
        oev_sched[chunks[-1]] = ndv
        nsc += 16
        ost_sched[chunks[-1]] = nsc
        l_sem = {"d": s_dve, "g": s_gp}

        def _emit_qt(gp, cj):
            hl2, qk2 = chunks[cj]
            r0 = (qk2 // 2) * 256 + hl2 * 128
            mk = d_pa[cj % 2].inc(gp.dma_start(
                out=QT_sb[cj % 2][:, :],
                in_=agq_out[r0:r0 + 128, (qk2 % 2) * QC:(qk2 % 2) * QC + QC]), 16)
            assert mk == qt_ld_sched[(hl2, qk2)]

        @block.gpsimd
        def _(gp):
            gp.wait_ge(s_pe.h, vtr_sched[kvt[-1]])
            gp.wait_ge(s_cc.h, cc_q)
            _emit_qt(gp, 0)
            _emit_qt(gp, 1)
            def _emit_ost(cj):
                hl2, qk2 = chunks[cj]
                gp.wait_ge(s_dve.h, oev_sched[(hl2, qk2)])
                r0 = (qk2 // 2) * 256 + hl2 * 128
                mk = gp.dma_start(
                    out=ago_in[r0:r0 + 128, (qk2 % 2) * QC:(qk2 % 2) * QC + QC],
                    in_=o_sb[:, :])
                assert d_scal.inc(mk, 16) == ost_sched[(hl2, qk2)]

            for ci, (hl, qk) in enumerate(chunks):
                nb = 4 * qk + 4
                first = True
                for kb in range(nb):
                    if not _on_gp(kb):
                        continue
                    gi = gidx[(hl, qk, kb)]
                    j = kb - 4 * qk
                    u0 = 128 * j if j >= 0 else 0
                    gp.wait_ge(s_act.h, exp_sched[(hl, qk, kb)])
                    if first:
                        if ci >= 1:
                            gp.wait_ge(s_pe.h, lsum_sched[chunks[ci - 1]])
                        if u0 > 0:
                            gp.memset(lb_sb[:, 0:u0], 0.0)
                        mk = gp.tensor_copy(lb_sb[:, u0:QC], PT_sb[gi % DEP][:, u0:QC])
                        first = False
                    else:
                        mk = gp.tensor_tensor(lb_sb[:, u0:QC], lb_sb[:, u0:QC],
                                              PT_sb[gi % DEP][:, u0:QC], ALU.add)
                    assert s_gp.inc(mk, 1) == l_sched[(hl, qk, kb)][1]
                if ci + 2 < len(chunks):
                    gp.wait_ge(s_pe.h, mmS_sched[(hl, qk, 4 * qk + 3)])
                    _emit_qt(gp, ci + 2)
                if ci >= 1:
                    _emit_ost(ci - 1)
            _emit_ost(len(chunks) - 1)

        @block.tensor
        def _(t):
            for gi in range(NG + DEP - 1):
                if gi < NG:
                    ci, hl, qk, kb = gb[gi]
                    j = kb - 4 * qk
                    u0 = 128 * j if j >= 0 else 0
                    if kb == 0:
                        t.wait_ge(d_pa[qt_par[(hl, qk)]].h, qt_ld_sched[(hl, qk)])
                        if ci == 0:
                            t.wait_ge(d_gp.h, kv_ld_sched[kvt[-1]])
                    if gi >= DEP:
                        _, phl, pqk, pkb = gb[gi - DEP]
                        t.wait_ge(s_act.h, exp_sched[(phl, pqk, pkb)])
                    mk = t.matmul(psS[gi % DEP][:, u0:QC],
                                  KT_sb[:, hl * S + kb * 128: hl * S + (kb + 1) * 128],
                                  QT_sb[ci % 2][:, u0:QC], start=True, stop=True)
                    assert s_pe.inc(mk, 1) == mmS_sched[(hl, qk, kb)]
                pj = gi - (DEP - 1)
                if pj < 0:
                    continue
                pci, phl, pqk, pkb = gb[pj]
                pnb = 4 * pqk + 4
                pj0 = 128 * (pkb - 4 * pqk) if pkb >= 4 * pqk else 0
                t.wait_ge(s_act.h, exp_sched[(phl, pqk, pkb)])
                if pkb == 0 and pci >= 2:
                    t.wait_ge(s_dve.h, oev_sched[chunks[pci - 2]])
                mk = t.matmul(psO[pci % 2][:, pj0:QC],
                              V_sb[:, phl * S + pkb * 128: phl * S + (pkb + 1) * 128],
                              PT_sb[pj % DEP][:, pj0:QC],
                              start=(pkb == 0), stop=(pkb == pnb - 1))
                assert s_pe.inc(mk, 1) == pvl_sched[(phl, pqk, pkb)]
                if pkb == pnb - 1:
                    t.wait_ge(s_dve.h, l_sched[(phl, pqk, l_last[(phl, pqk)]["d"])][1])
                    t.wait_ge(s_gp.h, l_sched[(phl, pqk, l_last[(phl, pqk)]["g"])][1])
                    if pci >= 1:
                        t.wait_ge(s_dve.h, rec_sched[chunks[pci - 1]])
                    t.matmul(psR[:, :], ones_ff[:, :], la_sb[:, :], start=True, stop=False)
                    mk = t.matmul(psR[:, :], ones_ff[:, :], lb_sb[:, :], start=False, stop=True)
                    assert s_pe.inc(mk, 2) == lsum_sched[(phl, pqk)]

        @block.vector
        def _(v):
            def _rec_oev(cj):
                hl2, qk2 = chunks[cj]
                v.wait_ge(s_pe.h, lsum_sched[(hl2, qk2)])
                mk = v.reciprocal(R_sb[:, :], psR[:, :])
                assert s_dve.inc(mk, 1) == rec_sched[(hl2, qk2)]
                if cj >= 1:
                    v.wait_ge(d_scal.h, ost_sched[chunks[cj - 1]])
                mk = v.tensor_tensor(o_sb[:, :], psO[cj % 2][:, :], R_sb[:, :], ALU.mult)
                assert s_dve.inc(mk, 1) == oev_sched[(hl2, qk2)]

            for ci, (hl, qk) in enumerate(chunks):
                nb = 4 * qk + 4
                first = True
                for kb in range(nb):
                    gi = gidx[(hl, qk, kb)]
                    if kb >= 4 * qk:
                        u0 = 128 * (kb - 4 * qk)
                        v.wait_ge(s_pe.h, mmS_sched[(hl, qk, kb)])
                        mk = v.tensor_tensor(psS[gi % DEP][:, u0:u0 + 128],
                                             psS[gi % DEP][:, u0:u0 + 128], tri_sb[:, :], ALU.add)
                        assert s_dve.inc(mk, 1) == mask_sched[(hl, qk, kb)]
                    if not _on_gp(kb):
                        j = kb - 4 * qk
                        u0 = 128 * j if j >= 0 else 0
                        v.wait_ge(s_act.h, exp_sched[(hl, qk, kb)])
                        if first:
                            if ci >= 1:
                                v.wait_ge(s_pe.h, lsum_sched[chunks[ci - 1]])
                            if u0 > 0:
                                v.memset(la_sb[:, 0:u0], 0.0)
                            mk = v.tensor_copy(la_sb[:, u0:QC], PT_sb[gi % DEP][:, u0:QC])
                            first = False
                        else:
                            mk = v.tensor_tensor(la_sb[:, u0:QC], la_sb[:, u0:QC],
                                                 PT_sb[gi % DEP][:, u0:QC], ALU.add)
                        assert s_dve.inc(mk, 1) == l_sched[(hl, qk, kb)][1]
                    if kb == 2 and ci >= 1:
                        _rec_oev(ci - 1)
            _rec_oev(len(chunks) - 1)

        @block.scalar
        def _(sc):
            for gi in range(NG):
                ci, hl, qk, kb = gb[gi]
                j = kb - 4 * qk
                u0 = 128 * j if j >= 0 else 0
                sc.wait_ge(s_pe.h, mmS_sched[(hl, qk, kb)])
                if (hl, qk, kb) in mask_sched:
                    sc.wait_ge(s_dve.h, mask_sched[(hl, qk, kb)])
                if gi >= DEP:
                    _, phl, pqk, pkb = gb[gi - DEP]
                    sc.wait_ge(s_pe.h, pvl_sched[(phl, pqk, pkb)])
                    e, cnt = l_sched[(phl, pqk, pkb)]
                    sc.wait_ge(l_sem[e].h, cnt)
                mk = sc.activation(PT_sb[gi % DEP][:, u0:QC], psS[gi % DEP][:, u0:QC],
                                   AF.Exp, bias=expb_sb[:, :], scale=SCALE)
                assert s_act.inc(mk, 1) == exp_sched[(hl, qk, kb)]

        if DEBUG:
            @block.scalar
            def _(sc):
                sc.wait_ge(s_dve.h, vev_sched[kvt[-1]])
                d_scal.inc(sc.dma_start(out=taps["kt"][:, :], in_=KT_sb[:, :]), 16)
                d_scal.inc(sc.dma_start(out=taps["vt"][:, :], in_=V_sb[:, :]), 16)

        @block.gpsimd
        def _(gp):
            gp.wait_ge(d_scal.h, ost_sched[chunks[-1]])
            with allow_ici_all_to_all():
                s_cc.inc(gp.collective_compute(
                    "AllToAll", ALU.bypass, replica_groups=[list(range(NCORE))],
                    ins=[ago_in.ap().opt()], outs=[ago_out.ap().opt()]), 1)

        cc_o = s_cc.n
        at_ps.close()
        at_es.close()
        _sc_at.__exit__(None, None, None)

        if DEBUG:
            @block.scalar
            def _(sc):
                sc.wait_ge(s_cc.h, cc_o)
                d_scal.inc(sc.dma_start(out=taps["x1b"][:, :], in_=xT[:, :]), 16)

        _sc_op = nc.named_scope("ph_outproj"); _sc_op.__enter__()
        op_es = ExitStack()
        OR_sb = op_es.enter_context(nc.sbuf_tensor("OR_sb", [128, ET * Q], BF16))
        wo_buf = [op_es.enter_context(nc.sbuf_tensor(f"wob{i}", [128, ET * 128], BF16)) for i in range(2)]
        op_ps = ExitStack()
        psY2 = [[op_ps.enter_context(nc.psum_tensor(f"psY2{i}{qc}", [128, QC], F32))
                 for qc in range(2)] for i in range(2)]

        or_ld_sched = {ft: d_gp.n + 16 * (ft + 1) for ft in range(ET)}
        wo_ld_sched = {}
        _pw = [d_pa[0].n, d_pa[1].n]
        for eo in range(ET):
            _pw[eo % 2] += 16
            wo_ld_sched[eo] = _pw[eo % 2]
        mmO_sched = {eo: s_pe.n + eo + 1 for eo in range(ET)}
        sttO_sched = {eo: s_dve.n + 2 * (eo + 1) for eo in range(ET)}

        @block.gpsimd
        def _(gp):
            gp.wait_ge(s_cc.h, cc_o)
            for ft in range(ET):
                mk = d_gp.inc(gp.dma_start(
                    out=OR_sb[:, ft * Q:(ft + 1) * Q],
                    in_=ago_out[ft * 128:(ft + 1) * 128, :]), 16)
                assert mk == or_ld_sched[ft]
            for eo in range(ET):
                if eo >= 2:
                    gp.wait_ge(s_pe.h, mmO_sched[eo - 2])
                mk = d_pa[eo % 2].inc(gp.dma_start(
                    out=wo_buf[eo % 2][:, :].rearrange("p (e c) -> p e c", c=128),
                    in_=woT[:, eo * 128:(eo + 1) * 128].rearrange("(t p) c -> p t c", p=128)), 16)
                assert mk == wo_ld_sched[eo]

        @block.tensor
        def _(t):
            for eo in range(ET):
                t.wait_ge(d_pa[eo % 2].h, wo_ld_sched[eo])
                if eo == 0:
                    t.wait_ge(d_gp.h, or_ld_sched[ET - 1])
                if eo >= 2:
                    t.wait_ge(s_dve.h, sttO_sched[eo - 2])
                last = None
                for qc in range(2):
                    for ft in range(ET):
                        last = t.matmul(psY2[eo % 2][qc][:, :],
                                        wo_buf[eo % 2][:, ft * 128:(ft + 1) * 128],
                                        OR_sb[:, ft * Q + qc * QC: ft * Q + qc * QC + QC],
                                        start=(ft == 0), stop=(ft == ET - 1))
                assert s_pe.inc(last, 1) == mmO_sched[eo]

        @block.vector
        def _(v):
            for eo in range(ET):
                v.wait_ge(s_pe.h, mmO_sched[eo])
                mk = None
                for qc in range(2):
                    xsl = xT[:, eo * Q + qc * QC: eo * Q + qc * QC + QC]
                    mk = v.scalar_tensor_tensor(xsl, psY2[eo % 2][qc][:, :],
                                                bias_sb["o"][:, eo:eo + 1], xsl,
                                                ALU.add, ALU.add)
                assert s_dve.inc(mk, 2) == sttO_sched[eo]

        outproj_pe = s_pe.n
        outproj_stt = s_dve.n
        if DEBUG:
            @block.scalar
            def _(sc):
                sc.wait_ge(s_dve.h, outproj_stt)
                d_scal.inc(sc.dma_start(out=taps["x2"][:, :], in_=xT[:, :]), 16)

            @block.gpsimd
            def _(gp):
                gp.wait_ge(s_cc.h, s_cc.n)
                d_gp.inc(gp.dma_start(out=taps["orr"][:, :], in_=ago_out[:, :]), 16)
        op_ps.close()
        op_es.close()
        _sc_op.__exit__(None, None, None)

        ln_es3 = ExitStack()
        xln3 = ln_es3.enter_context(nc.sbuf_tensor("xln3", [128, ET * Q], BF16))
        with nc.named_scope("ph_ln3"):
            ln3_done = emit_ln("l3", xln3, x_ready_dve=outproj_stt,
                               x_ready_scal=(d_scal.n if DEBUG else None))
        with nc.named_scope("ph_ffn2"):
            ffn2_stt, _ = emit_ffn("f2", xln3, ln3_done, outproj_pe, None, None, None)
        ln_es3.close()

        @block.sync
        def _(sy):
            sy.wait_ge(s_dve.h, ffn2_stt)
            for g in range(4):
                d_sync.inc(sy.dma_start(
                    out=out_y[:, :].rearrange("(t p) q -> p t q", p=128)[:, 4 * g:4 * g + 4, :],
                    in_=xT[:, :].rearrange("p (t q) -> p t q", q=Q)[:, 4 * g:4 * g + 4, :]), 16)
            sy.wait_ge(d_sync.h, d_sync.n)
            sy.wait_ge(d_scal.h, d_scal.n)
            sy.wait_ge(d_gp.h, d_gp.n)
            sy.wait_ge(d_pa[0].h, d_pa[0].n)
            sy.wait_ge(d_pa[1].h, d_pa[1].n)
            sy.wait_ge(d_sw.h, d_sw.n)
            sy.wait_ge(d_stq[0].h, d_stq[0].n)
            sy.wait_ge(d_stq[1].h, d_stq[1].n)

    es.close()
    return nc


def _host_prep(inputs):
    x = np.asarray(inputs["L0_attn_x"])[0]
    Wq = np.asarray(inputs["L0_attn_Wq"])
    Wk = np.asarray(inputs["L0_attn_Wk"])
    Wv = np.asarray(inputs["L0_attn_Wv"])
    Wo = np.asarray(inputs["L0_attn_Wo"])
    bq = np.asarray(inputs["L0_attn_bq"])
    bk = np.asarray(inputs["L0_attn_bk"])
    bv = np.asarray(inputs["L0_attn_bv"])
    bo = np.asarray(inputs["L0_attn_bo"])
    lut = np.asarray(inputs["L0_ffn_lut"])
    gidx = np.asarray(inputs["L0_ffn_gate_idx"])
    uidx = np.asarray(inputs["L0_ffn_up_idx"])
    didx = np.asarray(inputs["L0_ffn_down_idx"])

    j = np.arange(64)
    perm = np.empty(E, np.int64)
    for h in range(H):
        perm[h * D + j] = h * D + 2 * j
        perm[h * D + 64 + j] = h * D + 2 * j + 1

    wqT_a = np.ascontiguousarray(Wq[perm].T)
    wkT_a = np.ascontiguousarray(Wk[perm].T)
    wvT_a = np.ascontiguousarray(Wv.T)
    woT_a = np.ascontiguousarray(Wo.T)
    b_t = {
        "q": np.ascontiguousarray(bq[perm].reshape(16, 128).T),
        "k": np.ascontiguousarray(bk[perm].reshape(16, 128).T),
        "v": np.ascontiguousarray(bv.reshape(16, 128).T),
        "o": np.ascontiguousarray(bo.reshape(16, 128).T),
    }
    lutb = np.ascontiguousarray(np.broadcast_to(lut[None, :], (128, 16))).astype(np.float32)
    kl = np.arange(128)[:, None]
    u = np.arange(128)[None, :]
    tri = np.where(u >= kl, 0.0, NEG).astype(np.float32)
    rot = np.zeros((128, 128), np.float32)
    rot[np.arange(128), (np.arange(128) + 64) % 128] = 1.0
    rot = rot.astype(ml_dtypes.bfloat16)

    gidxT = gidx.T
    uidxT = uidx.T
    didxT = didx.T

    th = (10000.0 ** (-(np.arange(128) % 64) / 64.0))[:, None]
    in_maps = []
    for c in range(NCORE):
        pos = (c * Q + np.arange(Q))[None, :].astype(np.float64)
        ang = th * pos
        cosb = np.cos(ang)
        sinb = np.sin(ang)
        sinb[:64] *= -1.0
        in_maps.append({
            "xT_in": np.ascontiguousarray(x[c * Q:(c + 1) * Q, :].T),
            "wqT": wqT_a, "wkT": wkT_a, "wvT": wvT_a, "woT": woT_a,
            "bq_in": b_t["q"], "bk_in": b_t["k"], "bv_in": b_t["v"], "bo_in": b_t["o"],
            "lut_in": lutb,
            "gidx_in": np.ascontiguousarray(gidxT[256 * c:256 * (c + 1), :]),
            "uidx_in": np.ascontiguousarray(uidxT[256 * c:256 * (c + 1), :]),
            "didx_in": np.ascontiguousarray(didxT[688 * c:688 * (c + 1), :]),
            "cos_in": cosb.astype(ml_dtypes.bfloat16),
            "sin_in": sinb.astype(ml_dtypes.bfloat16),
            "tri_in": tri,
            "rot_in": rot,
        })
    return in_maps


def kernel(**inputs):
    in_maps = _host_prep(inputs)
    nc = build_nc()
    res = run_bass_kernel_spmd(nc, in_maps, core_ids=list(range(NCORE)))
    out = np.empty((1, S, E), np.float32)
    for c in range(NCORE):
        out[0, c * Q:(c + 1) * Q, :] = res.results[c]["out_y"].T
    return out
